# version 08 (final)
# speedup vs baseline: 1.7081x; 1.0469x over previous
import os
import numpy as np
from ml_dtypes import bfloat16
from contextlib import ExitStack

SAFE = int(os.environ.get("BASSKERN_SAFE", "0"))
GPMUL = int(os.environ.get("BASSKERN_GPMUL", "0"))

import concourse.bass as bass
import concourse.mybir as mybir
from concourse.ap import AP
from concourse.bass_utils import run_bass_kernel_spmd

F32 = mybir.dt.float32
BF16 = mybir.dt.bfloat16
I32 = mybir.dt.int32
AF = mybir.ActivationFunctionType
ALU = mybir.AluOpType
AXX = mybir.AxisListType.X

NP, D, S, H, HD = 128, 1024, 4096, 16, 64
NCH, NG, GT, PAIRS, W2 = 8, 8, 512, 8, 8192

ENGS = ("sync", "tensor", "scalar", "vector", "gpsimd")

OC_ONES = 0
OC_EA = 1
OC_MASK = 66
OC_EYE = 578
OC_MASK8 = 706
CONSTB_W = 706 + 240
QRM = 3072
RB0, RB1, RSA, RSB = 0, 4096, 8192, 8704
PZM, PZR = 9216, 10240

A_QP, A_KP, A_VT = 0, 8192, 16384
A_QSQ, A_VS = 20544, 21568
A_XT = 22592
A_ATR = 30784
A_ATF = 34880
A_EB = 38976
A_WST = 47168
A_RD = 50240
A_RBC = 51264
A_OPR = 52288
A_WOB = 38976
A_ZB, A_ZSQ = 47168, 48192
A_XN, A_XNW = 49216, 50240
F_XS, F_HB, F_WD = 0, 4096, 20480
F_WUB = 53248
F_SG = 57344
F_ZB, F_ZSQ = 58368, 59392
F_XT2 = 60416
F_XN, F_XNW = 55296, 56320
I_STG, I_SQ = 4096, 5120
DQ_O = 0
TN_S, TN_W = 0, 8192

ARB_W, ARF_W, ARI_W = 53312, 9216, 2048
ROWS_W, ROWSB_W = 11264, 5760


class Ch:
    def __init__(self, name):
        self.name = name
        self.n = 0


class Prog:
    def __init__(self):
        self.items = {e: [] for e in ENGS}
        self.chans = {}
        self._safe_marks = []
        self.retired = set()

    def ch(self, name):
        if name not in self.chans:
            self.chans[name] = Ch(name)
        return self.chans[name]

    def emit(self, eng, fn, waits=(), inc=None):
        if SAFE == 1 or (SAFE >= 2 and getattr(self, "_safe_on", False)):
            waits = list(waits) + getattr(self, "_safe_marks", [])
        seen = {}

        def _add(m):
            if m is None:
                return
            if isinstance(m, (list, tuple)) and m and isinstance(m[0], (list, tuple, type(None))):
                for x in m:
                    _add(x)
                return
            c, v = m
            if c is None or v <= 0:
                return
            if c.name in self.retired:
                return
            if c.name.startswith("D_"):
                v = c.n
            if c.name not in seen or seen[c.name] < v:
                seen[c.name] = v
        for m in waits:
            _add(m)
        ws = tuple(seen.items())
        if inc is not None:
            inc[0].n += inc[1]
        self.items[eng].append((ws, fn, None if inc is None else (inc[0].name, inc[1])))

    def build(self, handles, block):
        for eng in ENGS:
            items = self.items[eng]

            def body(e, items=items):
                for waits, fn, inc in items:
                    for nm, v in waits:
                        e.wait_ge(handles[nm], v)
                    ins = fn(e)
                    if inc is not None:
                        ins.then_inc(handles[inc[0]], inc[1])
            getattr(block, eng)(body)


class Rot:
    def __init__(self, n):
        self.sl = [dict(ready=None, free=None) for _ in range(n)]
        self.i = 0

    def next(self):
        s = self.sl[self.i % len(self.sl)]
        self.i += 1
        return s


def w(*marks):
    out = []
    for m in marks:
        if m is None:
            continue
        if isinstance(m, list):
            out += [x for x in m if x is not None]
        else:
            out.append(m)
    return out


class Kern:
    def __init__(self, launch):
        self.launch = launch
        self.P = Prog()
        P = self.P
        self.LD = P.ch("LD")
        self.LD2 = P.ch("LD2")
        self._ep = ""
        self.PE = P.ch("PE")
        self.SC = P.ch("SC")
        self.VE = P.ch("VE")
        self.GP = P.ch("GP")
        self.T = {}
        self.consts_mark = None
        self.stats_mark = None
        self._pbk2_free = None
        self._pbk1_free = None
        self._chain0_banks_free = None
        self._chain32_banks_free = None
        self._pa_bc_free = None
        self._zb_free = [None, None]
        self._xn_free = [None, None]
        self._xnw_free = [None, None]
        self._arfs_free = [None, None]
        self._arfs2_free = [None, None]
        self._arfo_free = [None, None]
        self._ffn_pbk_ev = [None, None]
        self._s2_free = None
        self._oall_done = None
        self._arfg_free = [None, None]
        self._arfg_i = 0

    def dma(self, q, dst_fn, src_fn, waits=(), site="G"):
        ch = self.P.ch("D_" + site)
        self.P.emit(q, lambda e, d=dst_fn, s=src_fn: e.dma_start(out=d(), in_=s()),
                    waits=waits, inc=(ch, 16))
        self._safe_update()
        return (ch, ch.n)

    def mm(self, out_fn, lhs_fn, rhs_fn, start, stop, waits=()):
        self.P.emit("tensor",
                    lambda e, o=out_fn, l=lhs_fn, r=rhs_fn, a=start, b=stop:
                    e.matmul(o(), l(), r(), start=a, stop=b),
                    waits=waits, inc=(self.PE, 1))
        self._safe_update()
        return (self.PE, self.PE.n)

    def tr(self, out_fn, in_fn, id_fn, waits=()):
        self.P.emit("tensor",
                    lambda e, o=out_fn, i=in_fn, d=id_fn: e.transpose(o(), i(), d()),
                    waits=waits, inc=(self.PE, 1))
        self._safe_update()
        return (self.PE, self.PE.n)

    def sc(self, fn, waits=()):
        self.P.emit("scalar", fn, waits=waits, inc=(self.SC, 1))
        self._safe_update()
        return (self.SC, self.SC.n)

    def ve(self, fn, waits=()):
        self.P.emit("vector", fn, waits=waits, inc=(self.VE, 1))
        self._safe_update()
        return (self.VE, self.VE.n)

    def gp(self, fn, waits=()):
        self.P.emit("gpsimd", fn, waits=waits, inc=(self.GP, 1))
        self._safe_update()
        return (self.GP, self.GP.n)

    def barrier(self):
        return [(c, c.n) for c in self.P.chans.values()]

    def _safe_update(self):
        if SAFE:
            self.P._safe_marks = self.barrier()

    def fence(self):
        marks = self.barrier()
        for eng in ENGS:
            self.P.emit(eng, lambda e: e.nop(), waits=marks, inc=None)

    def switch_epoch(self, tag):
        self.fence()
        for nm in ("PE", "SC", "VE", "GP"):
            nm2 = nm + self._ep
            if nm2 in self.P.chans:
                self.P.retired.add(nm2)
        self._ep = tag
        self.PE = self.P.ch("PE" + tag)
        self.SC = self.P.ch("SC" + tag)
        self.VE = self.P.ch("VE" + tag)
        self.GP = self.P.ch("GP" + tag)

    def ldgroup(self, src_fn, waits=(), wdt=4096, q="sync"):
        sl = self._arfg_i % 2
        self._arfg_i += 1
        ld = self.dma(q, lambda T=self.T, sl=sl, wdt=wdt:
                      T["ARF"][:, sl * 4096:sl * 4096 + wdt],
                      src_fn, waits=w(self._arfg_free[sl], *waits), site=f"AG{sl}")
        return sl, ld

    def emit_consts(self):
        self.dma("sync", lambda T=self.T: T["CONSTB"][:], lambda T=self.T: T["CONSTB_D"], site="C")
        self.dma("sync", lambda T=self.T: T["CONSTF"][:], lambda T=self.T: T["CONSTF_D"], site="C")
        self.dma("sync", lambda T=self.T: T["ONE1F"][:], lambda T=self.T: T["ONE1F_D"], site="C")
        m = self.dma("sync", lambda T=self.T: T["ONE1B"][:], lambda T=self.T: T["ONE1B_D"], site="C")
        self.consts_mark = m

    def emit_x_init(self):
        T = self.T
        arf_free = [None, None]
        arb_free = [None, None]
        for f in range(NCH):
            sl = f % 2
            ld = self.dma("sync",
                          lambda T=T, sl=sl: T["ARF"][:, sl * 4096:(sl + 1) * 4096],
                          lambda T=T, f=f: T["XN"][:, f, :],
                          waits=w(arf_free[sl], self.consts_mark), site=f"XI{sl}")
            cp = self.ve(lambda e, T=T, sl=sl:
                         e.tensor_copy(T["ARB"][:, sl * 4096:(sl + 1) * 4096],
                                       T["ARF"][:, sl * 4096:(sl + 1) * 4096]),
                         waits=w(ld, arb_free[sl]))
            arf_free[sl] = cp
            st = self.dma("gpsimd",
                          lambda T=T, f=f: T["XRES"][:, f, :],
                          lambda T=T, sl=sl: T["ARB"][:, sl * 4096:(sl + 1) * 4096],
                          waits=w(cp), site="XO")
            arb_free[sl] = st

    def emit_stats_pass(self, src_name, src_f32, chain, col_off=0):
        T = self.T
        ent = self.barrier()
        slot_free = [ent[2], ent[2]]
        chain_mark = None
        for g in range(NG):
            mmk = [None, None]
            for f in range(NCH):
                sl = (g * NCH + f) % 2
                if src_f32:
                    ld = self.dma("sync",
                                  lambda T=T, sl=sl: T["ARF"][:, 4096 + sl * 512:4096 + (sl + 1) * 512],
                                  lambda T=T, f=f, g=g, co=col_off:
                                  T[src_name][:, f, co + g * GT:co + (g + 1) * GT],
                                  waits=w(slot_free[sl], self.consts_mark), site=f"STT{sl}")
                    rdy = self.ve(lambda e, T=T, sl=sl:
                                  e.tensor_copy(T["ARB"][:, I_STG + sl * 512:I_STG + (sl + 1) * 512],
                                                T["ARF"][:, 4096 + sl * 512:4096 + (sl + 1) * 512]),
                                  waits=w(ld))
                else:
                    rdy = self.dma("sync",
                                   lambda T=T, sl=sl: T["ARB"][:, I_STG + sl * 512:I_STG + (sl + 1) * 512],
                                   lambda T=T, f=f, g=g, co=col_off:
                                   T[src_name][:, f, co + g * GT:co + (g + 1) * GT],
                                   waits=w(slot_free[sl], self.consts_mark), site=f"STT{sl}")
                sq = self.ve(lambda e, T=T, sl=sl:
                             e.tensor_mul(T["ARB"][:, I_SQ + sl * 512:I_SQ + (sl + 1) * 512],
                                          T["ARB"][:, I_STG + sl * 512:I_STG + (sl + 1) * 512],
                                          T["ARB"][:, I_STG + sl * 512:I_STG + (sl + 1) * 512]),
                             waits=w(rdy))
                m1 = self.mm(lambda T=T: T["PA"][0:1, 0:512],
                             lambda T=T: T["CONSTB"][:, OC_ONES:OC_ONES + 1],
                             lambda T=T, sl=sl: T["ARB"][:, I_STG + sl * 512:I_STG + (sl + 1) * 512],
                             start=(f == 0), stop=(f == NCH - 1),
                             waits=w(rdy, chain_mark if f == 0 else None))
                m2 = self.mm(lambda T=T: T["PA"][0:1, 512:1024],
                             lambda T=T: T["CONSTB"][:, OC_ONES:OC_ONES + 1],
                             lambda T=T, sl=sl: T["ARB"][:, I_SQ + sl * 512:I_SQ + (sl + 1) * 512],
                             start=(f == 0), stop=(f == NCH - 1),
                             waits=w(sq, chain_mark if f == 0 else None))
                slot_free[sl] = m2
                mmk = [m1, m2]
            first, last = self.emit_ln_chain(g, chain, mmk, row=0)
            chain_mark = first
        self.stats_mark = self.barrier()[4]

    def emit_stats_pass_g(self, src_name, ncols, chain, col_off=0):
        T = self.T
        ent = self.barrier()
        slot_free = [ent[2], ent[2]]
        chain_mark = None
        for g in range(NG):
            sl4, ld = self.ldgroup(lambda T=T, g=g, co=col_off, n=src_name, ncols=ncols:
                                   AP(T[n].tensor, co + g * GT,
                                      [[NCH * ncols, NP], [ncols, NCH], [1, GT]]),
                                   waits=w(self.consts_mark))
            mmk = [None, None]
            last_rdy = None
            for f in range(NCH):
                sl = (g * NCH + f) % 2
                rdy = self.sc(lambda e, T=T, sl=sl, sl4=sl4, f=f:
                              e.copy(T["ARB"][:, I_STG + sl * 512:I_STG + (sl + 1) * 512],
                                     T["ARF"][:, sl4 * 4096 + f * 512:
                                              sl4 * 4096 + (f + 1) * 512]),
                              waits=w(ld, slot_free[sl]))
                last_rdy = rdy
                sq = self.ve(lambda e, T=T, sl=sl:
                             e.tensor_mul(T["ARB"][:, I_SQ + sl * 512:I_SQ + (sl + 1) * 512],
                                          T["ARB"][:, I_STG + sl * 512:I_STG + (sl + 1) * 512],
                                          T["ARB"][:, I_STG + sl * 512:I_STG + (sl + 1) * 512]),
                             waits=w(rdy))
                m1 = self.mm(lambda T=T: T["PA"][0:1, 0:512],
                             lambda T=T: T["CONSTB"][:, OC_ONES:OC_ONES + 1],
                             lambda T=T, sl=sl: T["ARB"][:, I_STG + sl * 512:I_STG + (sl + 1) * 512],
                             start=(f == 0), stop=(f == NCH - 1),
                             waits=w(rdy, chain_mark if f == 0 else None))
                m2 = self.mm(lambda T=T: T["PA"][0:1, 512:1024],
                             lambda T=T: T["CONSTB"][:, OC_ONES:OC_ONES + 1],
                             lambda T=T, sl=sl: T["ARB"][:, I_SQ + sl * 512:I_SQ + (sl + 1) * 512],
                             start=(f == 0), stop=(f == NCH - 1),
                             waits=w(sq, chain_mark if f == 0 else None))
                slot_free[sl] = m2
                mmk = [m1, m2]
            self._arfg_free[sl4] = last_rdy
            first, last = self.emit_ln_chain(g, chain, mmk, row=0)
            chain_mark = first
        self.stats_mark = self.barrier()[4]

    def emit_ln_chain(self, g, chain, psum_marks, row, sx=0, sxx=512,
                      b0=None, b1=None, rm_off=None, rm_dst=None):
        T = self.T
        if b0 is None:
            b0 = RB0 + g * GT
        if b1 is None:
            b1 = RB1 + g * GT
        par = g % 2
        sa = RSA if par == 0 else RSB
        sb = RSB if par == 0 else RSA
        r = row
        mve = self.ve(lambda e, T=T: e.tensor_scalar(
            T["ROWS"][r:r + 1, b0:b0 + 512], T["PA"][r:r + 1, sx:sx + 512],
            1.0 / D, None, ALU.mult), waits=psum_marks)
        mmx = self.ve(lambda e, T=T: e.tensor_mul(
            T["ROWSB"][r:r + 1, 4096 + par * 512:4096 + (par + 1) * 512],
            T["ROWS"][r:r + 1, b0:b0 + 512], T["ROWS"][r:r + 1, b0:b0 + 512]),
            waits=w(mve))
        vv = self.ve(lambda e, T=T: e.scalar_tensor_tensor(
            T["ROWS"][r:r + 1, sa:sa + 512], T["PA"][r:r + 1, sxx:sxx + 512],
            1.0 / D,
            T["ROWSB"][r:r + 1, 4096 + par * 512:4096 + (par + 1) * 512],
            ALU.mult, ALU.subtract), waits=w(mmx))
        banks_free = vv
        sq = self.sc(lambda e, T=T: e.activation(
            T["ROWS"][r:r + 1, b1:b1 + 512], T["ROWS"][r:r + 1, sa:sa + 512],
            AF.Sqrt, bias=T["CONSTF"][r:r + 1, 0:1]), waits=w(vv))
        rc = self.ve(lambda e, T=T: e.reciprocal(
            T["ROWS"][r:r + 1, sa:sa + 512], T["ROWS"][r:r + 1, b1:b1 + 512]),
            waits=w(sq))
        if chain == "ln":
            last = self.ve(lambda e, T=T: e.tensor_copy(
                T["ROWS"][r:r + 1, b1:b1 + 512], T["ROWS"][r:r + 1, sa:sa + 512]),
                waits=w(rc))
        else:
            p1 = self.ve(lambda e, T=T: e.tensor_mul(
                T["ROWS"][r:r + 1, sb:sb + 512], T["ROWS"][r:r + 1, sa:sa + 512],
                T["ROWS"][r:r + 1, sa:sa + 512]), waits=w(rc))
            p2 = self.ve(lambda e, T=T: e.tensor_scalar(
                T["ROWS"][r:r + 1, b1:b1 + 512], T["ROWS"][r:r + 1, sb:sb + 512],
                -1e-5, 1.0, ALU.mult, ALU.add), waits=w(p1))
            p3 = self.sc(lambda e, T=T: e.activation(
                T["ROWS"][r:r + 1, b1:b1 + 512], T["ROWS"][r:r + 1, b1:b1 + 512],
                AF.Sqrt), waits=w(p2))
            p4 = self.ve(lambda e, T=T: e.tensor_scalar(
                T["ROWS"][r:r + 1, sb:sb + 512], T["ROWS"][r:r + 1, b1:b1 + 512],
                1e-8, None, ALU.add), waits=w(p3))
            p5 = self.ve(lambda e, T=T: e.reciprocal(
                T["ROWS"][r:r + 1, b1:b1 + 512], T["ROWS"][r:r + 1, sb:sb + 512]),
                waits=w(p4))
            last = self.ve(lambda e, T=T: e.tensor_mul(
                T["ROWS"][r:r + 1, b1:b1 + 512], T["ROWS"][r:r + 1, b1:b1 + 512],
                T["ROWS"][r:r + 1, sa:sa + 512]), waits=w(p5))
        if row == 0:
            ro = g * GT if rm_off is None else rm_off
            if rm_dst is None:
                rdst = lambda T=T, ro=ro: T["ROWSB"][0:1, ro:ro + 512]
            else:
                rdst = lambda T=T, ro=rm_dst: T["ARB"][0:1, ro:ro + 512]
            last = self.ve(lambda e, T=T, rd=rdst: e.scalar_tensor_tensor(
                rd(),
                T["ROWS"][0:1, b0:b0 + 512], -1.0,
                T["ROWS"][0:1, b1:b1 + 512], ALU.mult, ALU.mult),
                waits=w(last))
        return banks_free, last

    def emit_attention(self, sp):
        self.fence()
        T = self.T
        fold = sp["src_kind"] in ("f32_fold", "bf16_fold")
        srcf32 = sp["src_kind"] == "f32_fold"
        wst_free, wst_used = None, None
        xt_free = [None, None]
        arfx_free = [None, None]
        eb_free = None
        at_free = [None, None]
        atf_free = [None, None]
        opr_free = [None, None]
        rd_free = [None, None]
        rbc_free = [None, None]
        qsq_free = [None, None]
        vs_free = None
        sbias_free = None
        pa_st_free = None
        rstd_free = [None, None]
        pa_half_free = [None, None]
        at_free4 = [None, None, None, None]
        atf_free4 = [None, None, None, None]
        pa_n_free = [None, None, None, None]
        oall_marks = []
        prev_pe = None
        prev_sc = None
        prev_ve = None
        for p in range(PAIRS):
            wld = self.dma("sync", lambda T=T: T["ARF"][:, 4096:7168],
                           lambda T=T, n=sp["wq"], p=p: T[n][p],
                           waits=w(wst_free, self.consts_mark), site="WQ")
            bld = self.dma("sync", lambda T=T: T["SBIAS"][:, 0:3],
                           lambda T=T, n=sp["bq"], p=p: T[n][p],
                           waits=w(sbias_free), site="BQ")
            wcast = self.sc(lambda e, T=T: e.copy(T["ARB"][:, A_WST:A_WST + 3072],
                                                  T["ARF"][:, 4096:7168]),
                            waits=w(wld, wst_used))
            wst_free = wcast
            bsc = self.ve(lambda e, T=T: e.tensor_scalar(
                T["SBIAS"][:, 3:4], T["SBIAS"][:, 0:1], 0.125, None, ALU.mult),
                waits=w(bld))
            cs_ev = None
            if fold:
                cm = None
                for fc in range(NCH):
                    cm = self.mm(lambda T=T: T["PBK2"][0:1, 0:384],
                                 lambda T=T: T["CONSTB"][:, OC_ONES:OC_ONES + 1],
                                 lambda T=T, fc=fc:
                                 T["ARB"][:, A_WST + fc * 384:A_WST + (fc + 1) * 384],
                                 start=(fc == 0), stop=(fc == NCH - 1),
                                 waits=w(wcast, self._pbk2_free) if fc == 0 else ())
                cs_ev = self.ve(lambda e, T=T: e.tensor_copy(
                    T["ROWSB"][0:1, 5376:5760], T["PBK2"][0:1, 0:384]), waits=w(cm))
                self._pbk2_free = cs_ev
            kones = self.ve(lambda e, T=T: e.memset(T["ARB"][64:65, A_KP:A_KP + 8192], 1.0),
                            waits=w(prev_pe))
            vones = self.ve(lambda e, T=T: e.memset(
                AP(T["ARB"][:].tensor, A_VT + 64, [[ARB_W, NP], [65, 64]]), 1.0),
                waits=w(prev_pe))
            pbk_ev = [w(prev_pe, prev_ve), w(prev_pe, prev_ve)]
            kn_acc = [None, None]
            qn_evs = []
            ptrb_free = [None, None, None, None]
            last_qkv_mm = None
            deferred2 = None
            def bcg2(g):
                bc = self.mm(lambda T=T: T["PBK2"][:, 0:512],
                             lambda T=T: T["ONE1F"][0:1, 0:NP],
                             lambda T=T, g=g:
                             T["ROWS"][0:1, RB1 + g * GT:RB1 + (g + 1) * GT],
                             start=True, stop=True,
                             waits=w(self._pbk2_free, self.stats_mark))
                rs = g % 2
                cpb = self.sc(lambda e, T=T, rs=rs:
                              e.copy(T["ARF"][:, 8192 + rs * 512:
                                              8192 + (rs + 1) * 512],
                                     T["PBK2"][:, 0:512]),
                              waits=w(bc, rstd_free[rs]))
                self._pbk2_free = cpb
                return cpb
            if fold and not srcf32:
                cpbs = [bcg2(0)]
            for g in range(NG):
                xsl = g % 2
                bc = None
                if srcf32:
                    bc = self.mm(lambda T=T: T["PBK2"][:, 0:512],
                                 lambda T=T: T["ONE1F"][0:1, 0:NP],
                                 lambda T=T, g=g:
                                 T["ROWS"][0:1, RB1 + g * GT:RB1 + (g + 1) * GT],
                                 start=True, stop=True,
                                 waits=w(self._pbk2_free, self.stats_mark))
                tile_marks = []
                for fc in range(NCH):
                    asl = fc % 2
                    xt_w = [xt_free[xsl]] if fc == 0 else []
                    if srcf32:
                        ld = self.dma("sync",
                                      lambda T=T, asl=asl:
                                      T["ARF"][:, 7168 + asl * 512:7168 + (asl + 1) * 512],
                                      lambda T=T, n=sp["src"], fc=fc, g=g:
                                      T[n][:, fc, g * GT:(g + 1) * GT],
                                      waits=w(arfx_free[asl],
                                              self.barrier() if SAFE == 7 else None),
                                      site=f"AXs{asl}")
                        mv = self.ve(lambda e, T=T, xsl=xsl, fc=fc, asl=asl:
                                     e.tensor_tensor(
                                         T["ARB"][:, A_XT + xsl * 4096 + fc * 512:
                                                  A_XT + xsl * 4096 + (fc + 1) * 512],
                                         T["ARF"][:, 7168 + asl * 512:7168 + (asl + 1) * 512],
                                         T["PBK2"][:, 0:512], ALU.mult),
                                     waits=w(ld, bc, *xt_w))
                        arfx_free[asl] = mv
                        tile_marks.append(mv)
                    elif fc == 0:
                        ld = self.dma("sync",
                                      lambda T=T, xsl=xsl:
                                      T["ARB"][:, A_XT + xsl * 4096:
                                               A_XT + xsl * 4096 + 4096],
                                      lambda T=T, n=sp["src"], g=g:
                                      AP(T[n].tensor, g * GT,
                                         [[NCH * S, NP], [S, NCH], [1, GT]]),
                                      waits=xt_w, site=f"AXd{xsl}")
                        tile_marks.append(ld)
                    else:
                        tile_marks.append(tile_marks[0])
                if fold and not srcf32:
                    rs = g % 2
                    mvs = []
                    for fc in range(NCH):
                        mv = self.ve(lambda e, T=T, xsl=xsl, fc=fc, rs=rs:
                                     e.tensor_tensor(
                                         T["ARB"][:, A_XT + xsl * 4096 + fc * 512:
                                                  A_XT + xsl * 4096 + (fc + 1) * 512],
                                         T["ARB"][:, A_XT + xsl * 4096 + fc * 512:
                                                  A_XT + xsl * 4096 + (fc + 1) * 512],
                                         T["ARF"][:, 8192 + rs * 512:8192 + (rs + 1) * 512],
                                         ALU.mult),
                                     waits=w(tile_marks[fc], cpbs[g]))
                        mvs.append(mv)
                    tile_marks = mvs
                    rstd_free[rs] = mvs[-1]
                    if g < NG - 1:
                        cpbs.append(bcg2(g + 1))
                elif fold:
                    self._pbk2_free = tile_marks[-1]
                if deferred2 is not None:
                    deferred2()
                    deferred2 = None
                mmk_of = {}
                for mc in range(3):
                    bi = mc % 2
                    mmk = None
                    for fc in range(NCH):
                        mmk = self.mm(lambda T=T, bi=bi: T[f"PBK{bi}"][:, 0:512],
                                      lambda T=T, fc=fc, mc=mc:
                                      T["ARB"][:, A_WST + fc * 384 + mc * 128:
                                               A_WST + fc * 384 + (mc + 1) * 128],
                                      lambda T=T, xsl=xsl, fc=fc:
                                      T["ARB"][:, A_XT + xsl * 4096 + fc * 512:
                                               A_XT + xsl * 4096 + (fc + 1) * 512],
                                      start=(fc == 0), stop=(fc == NCH - 1 and not fold),
                                      waits=w(tile_marks[-1] if fc == 0 else None,
                                              wcast if fc == 0 else None,
                                              bsc if fc == 0 else None,
                                              pbk_ev[bi] if fc == 0 else None))
                    if fold:
                        mmk = self.mm(lambda T=T, bi=bi: T[f"PBK{bi}"][:, 0:512],
                                      lambda T=T, mc=mc:
                                      T["ROWSB"][0:1, 5376 + mc * 128:5376 + (mc + 1) * 128],
                                      lambda T=T, g=g: T["ROWSB"][0:1, g * GT:(g + 1) * GT],
                                      start=False, stop=True,
                                      waits=w(cs_ev, self.stats_mark))
                    mmk_of[mc] = mmk
                    if mc == 2:
                        xt_free[xsl] = mmk
                        last_qkv_mm = mmk
                    if mc == 0:
                        eA = self.ve(lambda e, T=T, g=g:
                                     e.tensor_scalar(T["ARB"][0:64, A_QP + g * GT:A_QP + (g + 1) * GT],
                                                     T["PBK0"][0:64, 0:512],
                                                     0.125, T["SBIAS"][0:64, 3:4],
                                                     ALU.mult, ALU.add),
                                     waits=w(mmk))
                        eB = self.ve(lambda e, T=T, g=g:
                                     e.tensor_scalar(T["ARB"][0:64, A_QP + 4096 + g * GT:
                                                              A_QP + 4096 + g * GT + 512],
                                                     T["PBK0"][64:128, 0:512],
                                                     0.125, T["SBIAS"][64:128, 3:4],
                                                     ALU.mult, ALU.add),
                                     waits=w(mmk))
                        pbk_ev[0] = eB
                        for h2, em in ((0, eA), (1, eB)):
                            sqm = self.ve(lambda e, T=T, g=g, h2=h2:
                                          e.tensor_mul(
                                              T["ARB"][0:64, A_QSQ + h2 * 512:A_QSQ + (h2 + 1) * 512],
                                              T["ARB"][0:64, A_QP + h2 * 4096 + g * GT:
                                                       A_QP + h2 * 4096 + g * GT + 512],
                                              T["ARB"][0:64, A_QP + h2 * 4096 + g * GT:
                                                       A_QP + h2 * 4096 + g * GT + 512]),
                                          waits=w(em, qsq_free[h2]))
                            off = h2 * 512
                            nmk = self.mm(lambda T=T, off=off: T["PA"][0:65, off:off + 512],
                                          lambda T=T: T["CONSTB"][0:64, OC_EA:OC_EA + 65],
                                          lambda T=T, h2=h2:
                                          T["ARB"][0:64, A_QSQ + h2 * 512:A_QSQ + (h2 + 1) * 512],
                                          start=True, stop=True,
                                          waits=w(sqm, pa_n_free[h2], prev_sc))
                            qsq_free[h2] = nmk
                            ev = self.ve(lambda e, T=T, g=g, h2=h2, off=off:
                                         e.tensor_copy(
                                             T["ROWS"][64:65, h2 * 4096 + g * GT:
                                                       h2 * 4096 + (g + 1) * GT],
                                             T["PA"][64:65, off:off + 512]),
                                         waits=w(nmk))
                            pa_n_free[h2] = ev
                            qn_evs.append(ev)
                def mk_def2(g, mmk1, mmk2, xsl):
                    def emit():
                        nonlocal vs_free
                        eA = self.ve(lambda e, T=T, g=g:
                                     e.tensor_scalar(T["ARB"][0:64, A_KP + g * GT:A_KP + (g + 1) * GT],
                                                     T["PBK1"][0:64, 0:512],
                                                     1.0, T["SBIAS"][0:64, 1:2],
                                                     ALU.mult, ALU.add),
                                     waits=w(mmk1, kones))
                        eB = self.ve(lambda e, T=T, g=g:
                                     e.tensor_scalar(T["ARB"][0:64, A_KP + 4096 + g * GT:
                                                              A_KP + 4096 + g * GT + 512],
                                                     T["PBK1"][64:128, 0:512],
                                                     1.0, T["SBIAS"][64:128, 1:2],
                                                     ALU.mult, ALU.add),
                                     waits=w(mmk1))
                        pbk_ev[1] = eB
                        for h2, em in ((0, eA), (1, eB)):
                            sqm = self.ve(lambda e, T=T, g=g, h2=h2:
                                          e.tensor_mul(
                                              T["ARB"][0:64, A_QSQ + h2 * 512:A_QSQ + (h2 + 1) * 512],
                                              T["ARB"][0:64, A_KP + h2 * 4096 + g * GT:
                                                       A_KP + h2 * 4096 + g * GT + 512],
                                              T["ARB"][0:64, A_KP + h2 * 4096 + g * GT:
                                                       A_KP + h2 * 4096 + g * GT + 512]),
                                          waits=w(em, qsq_free[h2]))
                            off = 1024 + h2 * 512
                            ni = 2 + h2
                            nmk = self.mm(lambda T=T, off=off: T["PA"][0:65, off:off + 512],
                                          lambda T=T: T["CONSTB"][0:64, OC_EA:OC_EA + 65],
                                          lambda T=T, h2=h2:
                                          T["ARB"][0:64, A_QSQ + h2 * 512:A_QSQ + (h2 + 1) * 512],
                                          start=True, stop=True,
                                          waits=w(sqm, pa_n_free[ni], prev_sc))
                            qsq_free[h2] = nmk
                            r1 = self.ve(lambda e, T=T, off=off, h2=h2:
                                         e.tensor_reduce(T["COLS"][64:65, 4 + h2:5 + h2],
                                                         T["PA"][64:65, off:off + 512],
                                                         AXX, ALU.max),
                                         waits=w(nmk, kn_acc[h2]))
                            if g == 0:
                                r2 = self.ve(lambda e, T=T, h2=h2:
                                             e.tensor_copy(T["COLS"][64:65, h2:h2 + 1],
                                                           T["COLS"][64:65, 4 + h2:5 + h2]),
                                             waits=w(r1))
                            else:
                                r2 = self.ve(lambda e, T=T, h2=h2:
                                             e.tensor_max(T["COLS"][64:65, h2:h2 + 1],
                                                          T["COLS"][64:65, h2:h2 + 1],
                                                          T["COLS"][64:65, 4 + h2:5 + h2]),
                                             waits=w(r1))
                            pa_n_free[ni] = r1
                            kn_acc[h2] = r2
                        ev = self.ve(lambda e, T=T:
                                     e.tensor_scalar(T["ARB"][:, A_VS:A_VS + 512],
                                                     T["PBK0"][:, 0:512],
                                                     T["SBIAS"][:, 2:3], None, ALU.add),
                                     waits=w(mmk2, vs_free))
                        pbk_ev[0] = ev
                        last_tr = None
                        for h2 in range(2):
                            tmk = None
                            for tc in range(4):
                                tmk = self.tr(lambda T=T, tc=tc:
                                              T["PTRB"][:, tc * 64:(tc + 1) * 64],
                                              lambda T=T, h2=h2, tc=tc:
                                              T["ARB"][h2 * 64:h2 * 64 + 64,
                                                       A_VS + tc * 128:A_VS + (tc + 1) * 128],
                                              lambda T=T, h2=h2:
                                              T["CONSTB"][h2 * 64:h2 * 64 + 64,
                                                          OC_EYE:OC_EYE + 64],
                                              waits=w(ev, ptrb_free[0] if tc == 0 else None))
                            cbase = h2 * 32 + g * 4
                            t2 = self.sc(lambda e, T=T, cbase=cbase:
                                         e.copy(AP(T["ARB"][:].tensor, A_VT + cbase * 65,
                                                   [[ARB_W, NP], [65, 4], [1, 64]]),
                                                AP(T["PTRB"][:].tensor, 0,
                                                   [[256, NP], [64, 4], [1, 64]])),
                                         waits=w(tmk, vones))
                            ptrb_free[0] = t2
                            last_tr = t2
                        vs_free = last_tr
                    return emit
                deferred2 = mk_def2(g, mmk_of[1], mmk_of[2], xsl)
            if deferred2 is not None:
                deferred2()
                deferred2 = None
            wst_used = last_qkv_mm
            sbias_free = pbk_ev[0]
            if sp.get("stage", 99) <= 2:
                prev_pe = (self.PE, self.PE.n)
                prev_sc = (self.SC, self.SC.n)
                prev_ve = (self.VE, self.VE.n)
                continue
            mt_last = None
            for g in range(NG):
                for h2 in range(2):
                    par = (g * 2 + h2) % 2
                    scr = 8192 + par * 512
                    t1 = self.ve(lambda e, T=T, g=g, h2=h2, scr=scr:
                                 e.tensor_scalar(T["ROWS"][64:65, scr:scr + 512],
                                                 T["ROWS"][64:65, h2 * 4096 + g * GT:
                                                           h2 * 4096 + (g + 1) * GT],
                                                 T["COLS"][64:65, h2:h2 + 1], None, ALU.mult),
                                 waits=w(kn_acc[h2], qn_evs[-1], mt_last))
                    t2 = self.sc(lambda e, T=T, scr=scr:
                                 e.activation(T["ROWS"][64:65, scr:scr + 512],
                                              T["ROWS"][64:65, scr:scr + 512], AF.Sqrt),
                                 waits=w(t1))
                    t3 = self.ve(lambda e, T=T, g=g, h2=h2, scr=scr:
                                 e.tensor_scalar(T["ARB"][64:65, A_QP + h2 * 4096 + g * GT:
                                                          A_QP + h2 * 4096 + g * GT + 512],
                                                 T["ROWS"][64:65, scr:scr + 512],
                                                 -1.0, None, ALU.mult),
                                 waits=w(t2))
                    mt_last = t3
            prep_done = mt_last
            if sp.get("dbg") and p == 0:
                bar = self.barrier()
                self.dma("gpsimd", lambda T=T: T["DBG1"][:, 0:24576],
                         lambda T=T: T["ARB"][:, 0:24576], waits=bar, site="DBG")
                self.dma("gpsimd", lambda T=T: T["DBG2"][:, 0:ROWS_W],
                         lambda T=T: T["ROWS"][:], waits=bar, site="DBG")
                self.dma("gpsimd", lambda T=T: T["DBG3"][:, 0:16],
                         lambda T=T: T["COLS"][:], waits=bar, site="DBG")
            if sp.get("stage", 99) <= 3:
                prev_pe = (self.PE, self.PE.n)
                prev_sc = (self.SC, self.SC.n)
                prev_ve = (self.VE, self.VE.n)
                continue
            if SAFE == 2:
                self.P._safe_on = True
            self._pbk1_free = pbk_ev[1]
            o_free = pbk_ev[0]
            for h2 in range(2):
                ebx = None
                if sp.get("eb"):
                    ebl = self.dma("sync",
                                   lambda T=T: T["ARB"][:, A_EB:A_EB + 8192],
                                   lambda T=T, n=sp["eb"], p=p, h2=h2: T[n][2 * p + h2],
                                   waits=w(eb_free), site="EB")
                    ebx = self.sc(lambda e, T=T:
                                  e.activation(T["ARB"][:, A_EB:A_EB + 8192],
                                               T["ARB"][:, A_EB:A_EB + 8192],
                                               AF.Exp),
                                  waits=w(ebl))
                gp_last = None
                def fin(h2, g, avm):
                    nonlocal o_free
                    osl = (h2 * NG + g) % 2
                    rd = self.ve(lambda e, T=T, osl=osl:
                                 e.reciprocal(T["ARB"][0:1, A_RD + osl * 512:
                                                       A_RD + (osl + 1) * 512],
                                              T["PBK0"][64:65, 0:512]),
                                 waits=w(avm, rd_free[osl]))
                    bcm = self.mm(lambda T=T: T["PBK1"][0:64, 0:512],
                                  lambda T=T: T["ONE1B"][0:1, 0:64],
                                  lambda T=T, osl=osl:
                                  T["ARB"][0:1, A_RD + osl * 512:A_RD + (osl + 1) * 512],
                                  start=True, stop=True,
                                  waits=w(rd, self._pbk1_free))
                    rd_free[osl] = bcm
                    rbc = self.ve(lambda e, T=T, osl=osl:
                                  e.tensor_copy(T["ARB"][0:64, A_RBC + osl * 512:
                                                         A_RBC + osl * 512 + 512],
                                                T["PBK1"][0:64, 0:512]),
                                  waits=w(bcm, rbc_free[osl]))
                    self._pbk1_free = rbc
                    omul = self.ve(lambda e, T=T, osl=osl:
                                   e.tensor_tensor(T["ARB"][0:64, A_OPR + osl * 512:
                                                            A_OPR + osl * 512 + 512],
                                                   T["PBK0"][0:64, 0:512],
                                                   T["ARB"][0:64, A_RBC + osl * 512:
                                                            A_RBC + osl * 512 + 512],
                                                   ALU.mult),
                                   waits=w(rbc, opr_free[osl]))
                    rbc_free[osl] = omul
                    o_free = omul
                    ost = self.dma("gpsimd",
                                   lambda T=T, h2=h2, p=p, g=g:
                                   T["OALL"][h2 * 64:(h2 + 1) * 64, p, g * GT:(g + 1) * GT],
                                   lambda T=T, osl=osl:
                                   T["ARB"][0:64, A_OPR + osl * 512:A_OPR + osl * 512 + 512],
                                   waits=w(omul), site="OS")
                    opr_free[osl] = ost
                    oall_marks.append(ost)
                if sp["diag"]:
                    for g in range(NG):
                        blocks = [[g * 4 + c for c in range(4)]]
                        nb = len(blocks)
                        avm = None
                        for ib, blk in enumerate(blocks):
                            atsl = (h2 * NG * nb + g * nb + ib) % 2
                            qm = None
                            for ci, j in enumerate(blk):
                                rhs = (lambda T=T, h2=h2, j=j:
                                       T["ARB"][0:65, A_QP + h2 * 4096 + j * 128:
                                                A_QP + h2 * 4096 + (j + 1) * 128])
                                out = (lambda T=T, ci=ci, g=g:
                                       T["PA"][:, (g % 4) * 512 + ci * 128:
                                               (g % 4) * 512 + (ci + 1) * 128])
                                qm = self.mm(out,
                                             lambda T=T, h2=h2, j=j:
                                             T["ARB"][0:65, A_KP + h2 * 4096 + j * 128:
                                                      A_KP + h2 * 4096 + (j + 1) * 128],
                                             rhs, start=True, stop=True,
                                             waits=w(prep_done, pa_st_free) if ci == 0
                                             else ())
                            expm = self.sc(lambda e, T=T, atsl=atsl, g=g:
                                           e.activation(T["ARB"][:, A_ATR + atsl * 2048:
                                                                 A_ATR + atsl * 2048 + 512],
                                                        T["PA"][:, (g % 4) * 512:
                                                                (g % 4) * 512 + 512], AF.Exp),
                                           waits=w(qm, at_free[atsl]))
                            pa_st_free = expm
                            mul = (self.gp if GPMUL else self.ve)(lambda e, T=T, atsl=atsl:
                                          e.tensor_tensor(T["ARB"][:, A_ATF + atsl * 2048:
                                                                   A_ATF + atsl * 2048 + 512],
                                                          T["ARB"][:, A_ATR + atsl * 2048:
                                                                   A_ATR + atsl * 2048 + 512],
                                                          T["CONSTB"][:, OC_MASK:OC_MASK + 512],
                                                          ALU.mult),
                                          waits=w(expm, atf_free[atsl]))
                            gp_last = mul
                            for ci, j in enumerate(blk):
                                out = (lambda T=T, ci=ci:
                                       T["PBK0"][0:65, ci * 128:(ci + 1) * 128])
                                rhs = (lambda T=T, atsl=atsl, ci=ci:
                                       T["ARB"][:, A_ATF + atsl * 2048 + ci * 128:
                                                A_ATF + atsl * 2048 + (ci + 1) * 128])
                                avm = self.mm(out,
                                              lambda T=T, h2=h2, j=j:
                                              T["ARB"][:, A_VT + (h2 * 32 + j) * 65:
                                                       A_VT + (h2 * 32 + j) * 65 + 65],
                                              rhs, start=True, stop=True,
                                              waits=w(mul, o_free if (ib == 0 and ci == 0)
                                                      else None))
                            at_free[atsl] = avm
                            atf_free[atsl] = avm
                        fin(h2, g, avm)
                else:
                    nb = 16
                    total = NG * nb
                    pend = None
                    for i in range(total + 1):
                        cur = None
                        if i < total:
                            g, ib = divmod(i, nb)
                            a2 = i % 2
                            a4 = i % 4
                            blk = [ib * 2, ib * 2 + 1]
                            qm = None
                            for ci, j in enumerate(blk):
                                qm = self.mm(lambda T=T, a2=a2, ci=ci:
                                             T["PA"][:, a2 * 1024 + ci * 512:
                                                     a2 * 1024 + (ci + 1) * 512],
                                             lambda T=T, h2=h2, j=j:
                                             T["ARB"][0:65, A_KP + h2 * 4096 + j * 128:
                                                      A_KP + h2 * 4096 + (j + 1) * 128],
                                             lambda T=T, h2=h2, g=g:
                                             T["ARB"][0:65, A_QP + h2 * 4096 + g * GT:
                                                      A_QP + h2 * 4096 + (g + 1) * GT],
                                             start=True, stop=True,
                                             waits=w(prep_done, pa_half_free[a2])
                                             if ci == 0 else ())
                            expm = self.sc(lambda e, T=T, a2=a2, a4=a4:
                                           e.activation(T["ARB"][:, A_ATR + a4 * 1024:
                                                                 A_ATR + (a4 + 1) * 1024],
                                                        T["PA"][:, a2 * 1024:
                                                                (a2 + 1) * 1024], AF.Exp),
                                           waits=w(qm, at_free4[a4]))
                            pa_half_free[a2] = expm
                            if sp.get("eb"):
                                mul = None
                                for ci, j in enumerate(blk):
                                    off = 4095 + g * GT - j * NP
                                    mul = (self.gp if GPMUL else self.ve)(
                                        lambda e, T=T, a4=a4, ci=ci, off=off:
                                        e.tensor_tensor(
                                            T["ARB"][:, A_ATF + a4 * 1024 + ci * 512:
                                                     A_ATF + a4 * 1024 + (ci + 1) * 512],
                                            T["ARB"][:, A_ATR + a4 * 1024 + ci * 512:
                                                     A_ATR + a4 * 1024 + (ci + 1) * 512],
                                            T["ARB"][:, A_EB + off:A_EB + off + 512],
                                            ALU.mult),
                                        waits=w(expm, ebx, atf_free4[a4] if ci == 0 else None))
                                cur = (g, ib, a4, A_ATF, mul)
                                gp_last = mul
                            else:
                                cur = (g, ib, a4, A_ATR, expm)
                        if pend is not None:
                            pg, pib, p4, prs, prm = pend
                            avm = None
                            for ci in range(2):
                                j = pib * 2 + ci
                                first = (pib == 0 and ci == 0)
                                last = (pib == nb - 1 and ci == 1)
                                avm = self.mm(lambda T=T: T["PBK0"][0:65, 0:512],
                                              lambda T=T, h2=h2, j=j:
                                              T["ARB"][:, A_VT + (h2 * 32 + j) * 65:
                                                       A_VT + (h2 * 32 + j) * 65 + 65],
                                              lambda T=T, p4=p4, ci=ci, rs=prs:
                                              T["ARB"][:, rs + p4 * 1024 + ci * 512:
                                                       rs + p4 * 1024 + (ci + 1) * 512],
                                              start=first, stop=last,
                                              waits=w(prm, o_free if first else None))
                            at_free4[p4] = avm
                            atf_free4[p4] = avm
                            if pib == nb - 1:
                                fin(h2, pg, avm)
                        pend = cur
                if sp.get("eb"):
                    eb_free = gp_last
            prev_pe = (self.PE, self.PE.n)
        if sp.get("stage", 99) <= 4:
            return
        self._oall_done = oall_marks[-1]
        self._emit_outproj(sp)

    def emit_attention_l2(self, sp):
        self.fence()
        T = self.T
        wst_free, wst_used = None, None
        xt_free = [None, None]
        at_free4 = [None, None, None, None]
        atf_free4 = [None, None, None, None]
        opr_free = [None, None]
        rd_free = [None, None]
        rbc_free = [None, None]
        qsq_free = [None, None]
        vs_free = None
        sbias_free = None
        pa_st2_free = [None, None]
        pa_n_free = [None, None, None, None]
        ptrb_free = None
        rstd_free = [None, None]
        qst_free = [None, None]
        oall_marks = []
        prev_pe = prev_sc = prev_ve = None
        for sect in range(8):
            self.emit_stats_pass_g("XF", 8 * S, chain="ln", col_off=sect * S)
            qmm = [None, None]
            qsl4, qld = self.ldgroup(lambda T=T, sect=sect:
                                     AP(T["XQ"].tensor, sect * GT,
                                        [[NCH * S, NP], [S, NCH], [1, GT]]))
            qrdy = None
            for fc in range(NCH):
                sl = fc % 2
                rdy = self.sc(lambda e, T=T, sl=sl, sl4=qsl4, fc=fc:
                              e.copy(T["ARB"][:, I_STG + sl * 512:I_STG + (sl + 1) * 512],
                                     T["ARF"][:, sl4 * 4096 + fc * 512:
                                              sl4 * 4096 + (fc + 1) * 512]),
                              waits=w(qld, qst_free[sl], self.stats_mark))
                qrdy = rdy
                sq = self.ve(lambda e, T=T, sl=sl:
                             e.tensor_mul(T["ARB"][:, I_SQ + sl * 512:I_SQ + (sl + 1) * 512],
                                          T["ARB"][:, I_STG + sl * 512:I_STG + (sl + 1) * 512],
                                          T["ARB"][:, I_STG + sl * 512:I_STG + (sl + 1) * 512]),
                             waits=w(rdy))
                m1 = self.mm(lambda T=T: T["PA"][0:1, 0:512],
                             lambda T=T: T["CONSTB"][:, OC_ONES:OC_ONES + 1],
                             lambda T=T, sl=sl: T["ARB"][:, I_STG + sl * 512:I_STG + (sl + 1) * 512],
                             start=(fc == 0), stop=(fc == NCH - 1), waits=w(rdy))
                m2 = self.mm(lambda T=T: T["PA"][0:1, 512:1024],
                             lambda T=T: T["CONSTB"][:, OC_ONES:OC_ONES + 1],
                             lambda T=T, sl=sl: T["ARB"][:, I_SQ + sl * 512:I_SQ + (sl + 1) * 512],
                             start=(fc == 0), stop=(fc == NCH - 1), waits=w(sq))
                qst_free[sl] = [rdy, m1, m2]
                qmm = [m1, m2]
            self._arfg_free[qsl4] = qrdy
            qbf, qlast = self.emit_ln_chain(0, "ln", qmm, row=0, b0=PZM, b1=PZR,
                                            rm_dst=QRM)
            qstats_mark = qlast
            for p in range(PAIRS):
                wsl, wld = self.ldgroup(lambda T=T, n=sp["wq"], p=p: T[n][p],
                                        waits=w(wst_free), wdt=3072)
                bld = self.dma("sync", lambda T=T: T["SBIAS"][:, 0:3],
                               lambda T=T, n=sp["bq"], p=p: T[n][p],
                               waits=w(sbias_free), site="BQ")
                wcast = self.sc(lambda e, T=T, wsl=wsl:
                                e.copy(T["ARB"][:, A_WST:A_WST + 3072],
                                       T["ARF"][:, wsl * 4096:wsl * 4096 + 3072]),
                                waits=w(wld, wst_used))
                self._arfg_free[wsl] = wcast
                wst_free = wcast
                bsc = self.ve(lambda e, T=T: e.tensor_scalar(
                    T["SBIAS"][:, 3:4], T["SBIAS"][:, 0:1], 0.125, None, ALU.mult),
                    waits=w(bld))
                cm = None
                for fc in range(NCH):
                    cm = self.mm(lambda T=T: T["PBK2"][0:1, 0:384],
                                 lambda T=T: T["CONSTB"][:, OC_ONES:OC_ONES + 1],
                                 lambda T=T, fc=fc:
                                 T["ARB"][:, A_WST + fc * 384:A_WST + (fc + 1) * 384],
                                 start=(fc == 0), stop=(fc == NCH - 1),
                                 waits=w(wcast, self._pbk2_free) if fc == 0 else ())
                cs_ev = self.ve(lambda e, T=T: e.tensor_copy(
                    T["ROWSB"][0:1, 5376:5760], T["PBK2"][0:1, 0:384]), waits=w(cm))
                self._pbk2_free = cs_ev
                kones = self.ve(lambda e, T=T: e.memset(T["ARB"][64:65, A_KP:A_KP + 8192], 1.0),
                                waits=w(prev_pe))
                vones = self.ve(lambda e, T=T: e.memset(
                    AP(T["ARB"][:].tensor, A_VT + 64, [[ARB_W, NP], [65, 64]]), 1.0),
                    waits=w(prev_pe))
                pbk_ev = [w(prev_pe, prev_ve), w(prev_pe, prev_ve)]
                kn_acc = [None, None]
                deferred = None
                def bcg(g):
                    bc = self.mm(lambda T=T: T["PBK2"][:, 0:512],
                                 lambda T=T: T["ONE1F"][0:1, 0:NP],
                                 lambda T=T, g=g:
                                 T["ROWS"][0:1, RB1 + g * GT:RB1 + (g + 1) * GT],
                                 start=True, stop=True,
                                 waits=w(self._pbk2_free, self.stats_mark))
                    rs = g % 2
                    cpb = self.sc(lambda e, T=T, rs=rs:
                                  e.copy(T["ARF"][:, 8192 + rs * 512:
                                                  8192 + (rs + 1) * 512],
                                         T["PBK2"][:, 0:512]),
                                  waits=w(bc, rstd_free[rs]))
                    self._pbk2_free = cpb
                    return cpb
                cpbs = [bcg(0)]
                for g in range(NG):
                    xsl = g % 2
                    rs = g % 2
                    sl4, ld = self.ldgroup(lambda T=T, g=g, sect=sect:
                                           AP(T["XF"].tensor, sect * S + g * GT,
                                              [[NCH * 8 * S, NP], [8 * S, NCH], [1, GT]]))
                    tile_marks = []
                    for fc in range(NCH):
                        mv = self.ve(lambda e, T=T, xsl=xsl, fc=fc, sl4=sl4, rs=rs:
                                     e.tensor_tensor(
                                         T["ARB"][:, A_XT + xsl * 4096 + fc * 512:
                                                  A_XT + xsl * 4096 + (fc + 1) * 512],
                                         T["ARF"][:, sl4 * 4096 + fc * 512:
                                                  sl4 * 4096 + (fc + 1) * 512],
                                         T["ARF"][:, 8192 + rs * 512:8192 + (rs + 1) * 512],
                                         ALU.mult),
                                     waits=w(ld, cpbs[g], xt_free[xsl] if fc == 0 else None))
                        tile_marks.append(mv)
                    self._arfg_free[sl4] = tile_marks[-1]
                    rstd_free[rs] = tile_marks[-1]
                    if g < NG - 1:
                        cpbs.append(bcg(g + 1))
                    if deferred is not None:
                        deferred()
                    mmk_of = {}
                    for mc in (1, 2):
                        bi = mc % 2
                        mmk = None
                        for fc in range(NCH):
                            mmk = self.mm(lambda T=T, bi=bi: T[f"PBK{bi}"][:, 0:512],
                                          lambda T=T, fc=fc, mc=mc:
                                          T["ARB"][:, A_WST + fc * 384 + mc * 128:
                                                   A_WST + fc * 384 + (mc + 1) * 128],
                                          lambda T=T, xsl=xsl, fc=fc:
                                          T["ARB"][:, A_XT + xsl * 4096 + fc * 512:
                                                   A_XT + xsl * 4096 + (fc + 1) * 512],
                                          start=(fc == 0), stop=False,
                                          waits=w(tile_marks[-1] if fc == 0 else None,
                                                  wcast if fc == 0 else None,
                                                  bsc if fc == 0 else None,
                                                  pbk_ev[bi] if fc == 0 else None))
                        mmk = self.mm(lambda T=T, bi=bi: T[f"PBK{bi}"][:, 0:512],
                                      lambda T=T, mc=mc:
                                      T["ROWSB"][0:1, 5376 + mc * 128:5376 + (mc + 1) * 128],
                                      lambda T=T, g=g: T["ROWSB"][0:1, g * GT:(g + 1) * GT],
                                      start=False, stop=True,
                                      waits=w(cs_ev, self.stats_mark))
                        mmk_of[mc] = mmk
                        if mc == 2:
                            xt_free[xsl] = mmk
                    def mk_deferred(g, mmk1, mmk2):
                        def emit():
                            nonlocal vs_free, ptrb_free
                            eA = self.ve(lambda e, T=T, g=g:
                                         e.tensor_scalar(
                                             T["ARB"][0:64, A_KP + g * GT:A_KP + (g + 1) * GT],
                                             T["PBK1"][0:64, 0:512],
                                             1.0, T["SBIAS"][0:64, 1:2],
                                             ALU.mult, ALU.add),
                                         waits=w(mmk1, kones))
                            eB = self.ve(lambda e, T=T, g=g:
                                         e.tensor_scalar(
                                             T["ARB"][0:64, A_KP + 4096 + g * GT:
                                                      A_KP + 4096 + g * GT + 512],
                                             T["PBK1"][64:128, 0:512],
                                             1.0, T["SBIAS"][64:128, 1:2],
                                             ALU.mult, ALU.add),
                                         waits=w(mmk1))
                            pbk_ev[1] = eB
                            for h2, em in ((0, eA), (1, eB)):
                                sqm = self.ve(lambda e, T=T, g=g, h2=h2:
                                              e.tensor_mul(
                                                  T["ARB"][0:64, A_QSQ + h2 * 512:A_QSQ + (h2 + 1) * 512],
                                                  T["ARB"][0:64, A_KP + h2 * 4096 + g * GT:
                                                           A_KP + h2 * 4096 + g * GT + 512],
                                                  T["ARB"][0:64, A_KP + h2 * 4096 + g * GT:
                                                           A_KP + h2 * 4096 + g * GT + 512]),
                                              waits=w(em, qsq_free[h2]))
                                off = 1024 + h2 * 512
                                ni = 2 + h2
                                nmk = self.mm(lambda T=T, off=off: T["PA"][0:65, off:off + 512],
                                              lambda T=T: T["CONSTB"][0:64, OC_EA:OC_EA + 65],
                                              lambda T=T, h2=h2:
                                              T["ARB"][0:64, A_QSQ + h2 * 512:A_QSQ + (h2 + 1) * 512],
                                              start=True, stop=True,
                                              waits=w(sqm, pa_n_free[ni], prev_sc))
                                qsq_free[h2] = nmk
                                r1 = self.ve(lambda e, T=T, off=off, h2=h2:
                                             e.tensor_reduce(T["COLS"][64:65, 4 + h2:5 + h2],
                                                             T["PA"][64:65, off:off + 512],
                                                             AXX, ALU.max),
                                             waits=w(nmk, kn_acc[h2]))
                                if g == 0:
                                    r2 = self.ve(lambda e, T=T, h2=h2:
                                                 e.tensor_copy(T["COLS"][64:65, h2:h2 + 1],
                                                               T["COLS"][64:65, 4 + h2:5 + h2]),
                                                 waits=w(r1))
                                else:
                                    r2 = self.ve(lambda e, T=T, h2=h2:
                                                 e.tensor_max(T["COLS"][64:65, h2:h2 + 1],
                                                              T["COLS"][64:65, h2:h2 + 1],
                                                              T["COLS"][64:65, 4 + h2:5 + h2]),
                                                 waits=w(r1))
                                pa_n_free[ni] = r1
                                kn_acc[h2] = r2
                            ev = self.ve(lambda e, T=T:
                                         e.tensor_scalar(T["ARB"][:, A_VS:A_VS + 512],
                                                         T["PBK0"][:, 0:512],
                                                         T["SBIAS"][:, 2:3], None, ALU.add),
                                         waits=w(mmk2, vs_free))
                            pbk_ev[0] = ev
                            last_tr = None
                            for h2 in range(2):
                                tmk = None
                                for tc in range(4):
                                    tmk = self.tr(lambda T=T, tc=tc:
                                                  T["PTRB"][:, tc * 64:(tc + 1) * 64],
                                                  lambda T=T, h2=h2, tc=tc:
                                                  T["ARB"][h2 * 64:h2 * 64 + 64,
                                                           A_VS + tc * 128:A_VS + (tc + 1) * 128],
                                                  lambda T=T, h2=h2:
                                                  T["CONSTB"][h2 * 64:h2 * 64 + 64,
                                                              OC_EYE:OC_EYE + 64],
                                                  waits=w(ev, ptrb_free if tc == 0 else None))
                                cbase = h2 * 32 + g * 4
                                t2 = self.sc(lambda e, T=T, cbase=cbase:
                                             e.copy(AP(T["ARB"][:].tensor, A_VT + cbase * 65,
                                                       [[ARB_W, NP], [65, 4], [1, 64]]),
                                                    AP(T["PTRB"][:].tensor, 0,
                                                       [[256, NP], [64, 4], [1, 64]])),
                                             waits=w(tmk, vones))
                                ptrb_free = t2
                                last_tr = t2
                            vs_free = last_tr
                        return emit
                    deferred = mk_deferred(g, mmk_of[1], mmk_of[2])
                deferred()
                deferred = None
                qbc = self.mm(lambda T=T: T["PBK2"][:, 0:512],
                              lambda T=T: T["ONE1F"][0:1, 0:NP],
                              lambda T=T: T["ROWS"][0:1, PZR:PZR + 512],
                              start=True, stop=True,
                              waits=w(self._pbk2_free, qstats_mark))
                sl4, ld = self.ldgroup(lambda T=T, sect=sect:
                                       AP(T["XQ"].tensor, sect * GT,
                                          [[NCH * S, NP], [S, NCH], [1, GT]]))
                qtiles = []
                for fc in range(NCH):
                    mv = self.ve(lambda e, T=T, fc=fc, sl4=sl4:
                                 e.tensor_tensor(
                                     T["ARB"][:, A_XT + fc * 512:A_XT + (fc + 1) * 512],
                                     T["ARF"][:, sl4 * 4096 + fc * 512:
                                              sl4 * 4096 + (fc + 1) * 512],
                                     T["PBK2"][:, 0:512], ALU.mult),
                                 waits=w(ld, qbc, xt_free[0] if fc == 0 else None))
                    qtiles.append(mv)
                self._arfg_free[sl4] = qtiles[-1]
                self._pbk2_free = qtiles[-1]
                mmk = None
                for fc in range(NCH):
                    mmk = self.mm(lambda T=T: T["PBK0"][:, 0:512],
                                  lambda T=T, fc=fc:
                                  T["ARB"][:, A_WST + fc * 384:A_WST + fc * 384 + 128],
                                  lambda T=T, fc=fc:
                                  T["ARB"][:, A_XT + fc * 512:A_XT + (fc + 1) * 512],
                                  start=(fc == 0), stop=False,
                                  waits=w(qtiles[fc], pbk_ev[0] if fc == 0 else None))
                mmk = self.mm(lambda T=T: T["PBK0"][:, 0:512],
                              lambda T=T: T["ROWSB"][0:1, 5376:5376 + 128],
                              lambda T=T: T["ARB"][0:1, QRM:QRM + 512],
                              start=False, stop=True,
                              waits=w(cs_ev, qstats_mark))
                xt_free[0] = mmk
                wst_used = mmk
                eA = self.ve(lambda e, T=T:
                             e.tensor_scalar(T["ARB"][0:64, A_QP:A_QP + 512],
                                             T["PBK0"][0:64, 0:512],
                                             0.125, T["SBIAS"][0:64, 3:4],
                                             ALU.mult, ALU.add),
                             waits=w(mmk))
                eB = self.ve(lambda e, T=T:
                             e.tensor_scalar(T["ARB"][0:64, A_QP + 4096:A_QP + 4096 + 512],
                                             T["PBK0"][64:128, 0:512],
                                             0.125, T["SBIAS"][64:128, 3:4],
                                             ALU.mult, ALU.add),
                             waits=w(mmk))
                pbk_ev[0] = eB
                mt_last = None
                for h2, em in ((0, eA), (1, eB)):
                    sqm = self.ve(lambda e, T=T, h2=h2:
                                  e.tensor_mul(T["ARB"][0:64, A_QSQ + h2 * 512:A_QSQ + (h2 + 1) * 512],
                                               T["ARB"][0:64, A_QP + h2 * 4096:A_QP + h2 * 4096 + 512],
                                               T["ARB"][0:64, A_QP + h2 * 4096:A_QP + h2 * 4096 + 512]),
                                  waits=w(em, qsq_free[h2]))
                    off = h2 * 512
                    nmk = self.mm(lambda T=T, off=off: T["PA"][0:65, off:off + 512],
                                  lambda T=T: T["CONSTB"][0:64, OC_EA:OC_EA + 65],
                                  lambda T=T, h2=h2:
                                  T["ARB"][0:64, A_QSQ + h2 * 512:A_QSQ + (h2 + 1) * 512],
                                  start=True, stop=True,
                                  waits=w(sqm, pa_n_free[h2], prev_sc))
                    qsq_free[h2] = nmk
                    scr = 8192 + h2 * 512
                    t1 = self.ve(lambda e, T=T, h2=h2, scr=scr, off=off:
                                 e.tensor_scalar(T["ROWS"][64:65, scr:scr + 512],
                                                 T["PA"][64:65, off:off + 512],
                                                 T["COLS"][64:65, h2:h2 + 1], None, ALU.mult),
                                 waits=w(nmk, kn_acc[h2], mt_last))
                    pa_n_free[h2] = t1
                    t2 = self.sc(lambda e, T=T, scr=scr:
                                 e.activation(T["ROWS"][64:65, scr:scr + 512],
                                              T["ROWS"][64:65, scr:scr + 512], AF.Sqrt),
                                 waits=w(t1))
                    t3 = self.ve(lambda e, T=T, h2=h2, scr=scr:
                                 e.tensor_scalar(T["ARB"][64:65, A_QP + h2 * 4096:
                                                          A_QP + h2 * 4096 + 512],
                                                 T["ROWS"][64:65, scr:scr + 512],
                                                 -1.0, None, ALU.mult),
                                 waits=w(t2))
                    mt_last = t3
                prep_done = mt_last
                self._pbk1_free = pbk_ev[1]
                o_free = pbk_ev[0]
                def fin2(h2, avm):
                    nonlocal o_free
                    osl = h2 % 2
                    rd = self.ve(lambda e, T=T, osl=osl:
                                 e.reciprocal(T["ARB"][0:1, A_RD + osl * 512:
                                                       A_RD + (osl + 1) * 512],
                                              T["PBK0"][64:65, 0:512]),
                                 waits=w(avm, rd_free[osl]))
                    bcm = self.mm(lambda T=T: T["PBK1"][0:64, 0:512],
                                  lambda T=T: T["ONE1B"][0:1, 0:64],
                                  lambda T=T, osl=osl:
                                  T["ARB"][0:1, A_RD + osl * 512:A_RD + (osl + 1) * 512],
                                  start=True, stop=True,
                                  waits=w(rd, self._pbk1_free))
                    rd_free[osl] = bcm
                    rbc = self.ve(lambda e, T=T, osl=osl:
                                  e.tensor_copy(T["ARB"][0:64, A_RBC + osl * 512:
                                                         A_RBC + osl * 512 + 512],
                                                T["PBK1"][0:64, 0:512]),
                                  waits=w(bcm, rbc_free[osl]))
                    self._pbk1_free = rbc
                    omul = self.ve(lambda e, T=T, osl=osl:
                                   e.tensor_tensor(T["ARB"][0:64, A_OPR + osl * 512:
                                                            A_OPR + osl * 512 + 512],
                                                   T["PBK0"][0:64, 0:512],
                                                   T["ARB"][0:64, A_RBC + osl * 512:
                                                            A_RBC + osl * 512 + 512],
                                                   ALU.mult),
                                   waits=w(rbc, opr_free[osl]))
                    rbc_free[osl] = omul
                    o_free = omul
                    ost = self.dma("gpsimd",
                                   lambda T=T, h2=h2, p=p, sect=sect:
                                   T["OALL"][h2 * 64:(h2 + 1) * 64, p,
                                             sect * GT:(sect + 1) * GT],
                                   lambda T=T, osl=osl:
                                   T["ARB"][0:64, A_OPR + osl * 512:A_OPR + osl * 512 + 512],
                                   waits=w(omul), site="OS")
                    opr_free[osl] = ost
                    oall_marks.append(ost)
                pend = None
                for idx in range(9):
                    cur = None
                    if idx < 8:
                        h2, qt = divmod(idx, 4)
                        a2 = idx % 2
                        a4 = idx % 4
                        qm = None
                        for j in range(8):
                            kc = qt * 8 + j
                            qm = self.mm(lambda T=T, j=j, a2=a2:
                                         T["PA"][:, a2 * 1024 + j * 128:
                                                 a2 * 1024 + (j + 1) * 128],
                                         lambda T=T, h2=h2, kc=kc:
                                         T["ARB"][0:65, A_KP + h2 * 4096 + kc * 128:
                                                  A_KP + h2 * 4096 + (kc + 1) * 128],
                                         lambda T=T, h2=h2, qt=qt:
                                         T["ARB"][0:65, A_QP + h2 * 4096 + qt * 128:
                                                  A_QP + h2 * 4096 + (qt + 1) * 128],
                                         start=True, stop=True,
                                         waits=w(prep_done,
                                                 pa_st2_free[a2] if j == 0 else None))
                        expm = self.sc(lambda e, T=T, a2=a2, a4=a4:
                                       e.activation(T["ARB"][:, A_ATR + a4 * 1024:
                                                             A_ATR + (a4 + 1) * 1024],
                                                    T["PA"][:, a2 * 1024:
                                                            (a2 + 1) * 1024], AF.Exp),
                                       waits=w(qm, at_free4[a4]))
                        pa_st2_free[a2] = expm
                        mul = None
                        for j in range(8):
                            mul = self.ve(lambda e, T=T, a4=a4, j=j:
                                          e.tensor_tensor(
                                              T["ARB"][:, A_ATF + a4 * 1024 + j * 128:
                                                       A_ATF + a4 * 1024 + (j + 1) * 128],
                                              T["ARB"][:, A_ATR + a4 * 1024 + j * 128:
                                                       A_ATR + a4 * 1024 + (j + 1) * 128],
                                              T["CONSTB"][:, OC_MASK8 + 112 - j * 16:
                                                          OC_MASK8 + 112 - j * 16 + 128],
                                              ALU.mult),
                                          waits=w(expm, atf_free4[a4] if j == 0 else None))
                        cur = (h2, qt, a4, mul)
                    if pend is not None:
                        ph2, pqt, p4, pmul = pend
                        avm = None
                        for j in range(8):
                            kc = pqt * 8 + j
                            avm = self.mm(lambda T=T, pqt=pqt:
                                          T["PBK0"][0:65, pqt * 128:(pqt + 1) * 128],
                                          lambda T=T, ph2=ph2, kc=kc:
                                          T["ARB"][:, A_VT + (ph2 * 32 + kc) * 65:
                                                   A_VT + (ph2 * 32 + kc) * 65 + 65],
                                          lambda T=T, p4=p4, j=j:
                                          T["ARB"][:, A_ATF + p4 * 1024 + j * 128:
                                                   A_ATF + p4 * 1024 + (j + 1) * 128],
                                          start=(j == 0), stop=(j == 7),
                                          waits=w(pmul, o_free
                                                  if (j == 0 and pqt == 0) else None))
                        at_free4[p4] = avm
                        atf_free4[p4] = avm
                        if pqt == 3:
                            fin2(ph2, avm)
                    pend = cur
                sbias_free = pbk_ev[0]
                prev_pe = (self.PE, self.PE.n)
                prev_sc = (self.SC, self.SC.n)
                prev_ve = (self.VE, self.VE.n)
        self._oall_done = oall_marks[-1]
        self._emit_outproj(sp)

    def _emit_outproj(self, sp):
        self.fence()
        T = self.T
        resid = sp["out"] == "resid"
        arf_free = [None, None]
        wob = []
        for kc in range(NCH):
            sl = kc % 2
            ld = self.dma("sync",
                          lambda T=T, sl=sl: T["ARF"][:, 4096 + sl * 1024:4096 + (sl + 1) * 1024],
                          lambda T=T, n=sp["wout"], kc=kc: T[n][:, kc, :],
                          waits=w(arf_free[sl]), site=f"WO{sl}")
            cp = self.sc(lambda e, T=T, kc=kc, sl=sl:
                         e.copy(T["ARB"][:, A_WOB + kc * 1024:A_WOB + (kc + 1) * 1024],
                                T["ARF"][:, 4096 + sl * 1024:4096 + (sl + 1) * 1024]),
                         waits=w(ld))
            arf_free[sl] = cp
            wob.append(cp)
        bld = self.dma("sync", lambda T=T: T["COLS"][:, 8:16],
                       lambda T=T, n=sp["bout"]: T[n], site="BO")
        og_free = [None, None]
        pbk_free = [None, None]
        zb_free = [None, None]
        xn_free = [None, None]
        xnw_free = [None, None]
        arfs_free = [None, None]
        arfs2_free = [None, None]
        arfo_free = [None, None]
        self._postln_regs = dict(zsq=A_ZSQ, xn=A_XN, xnw=A_XNW,
                                 zb_free=zb_free, xn_free=xn_free, xnw_free=xnw_free,
                                 arfs_free=arfs_free, arfs2_free=arfs2_free,
                                 arfo_free=arfo_free)
        self._pa_bc_free = None
        chain32_free = None
        zsb_free = None
        for g in range(NG):
            osl = g % 2
            ogld = self.dma(
                "sync",
                lambda T=T, osl=osl: T["ARB"][:, A_XT + osl * 4096:
                                              A_XT + osl * 4096 + 4096],
                lambda T=T, g=g: AP(T["OALL"].tensor, g * GT,
                                    [[NCH * S, NP], [S, NCH], [1, GT]]),
                waits=w(self._oall_done, og_free[osl]), site=f"OG{osl}")
            og_marks = [ogld] * NCH
            stats_mm = [None, None]
            ev_marks = []
            for ec in range(NCH):
                bi = ec % 2
                mmk = None
                for kc in range(NCH):
                    mmk = self.mm(lambda T=T, bi=bi: T[f"PBK{bi}"][:, 0:512],
                                  lambda T=T, kc=kc, ec=ec:
                                  T["ARB"][:, A_WOB + kc * 1024 + ec * 128:
                                           A_WOB + kc * 1024 + (ec + 1) * 128],
                                  lambda T=T, osl=osl, kc=kc:
                                  T["ARB"][:, A_XT + osl * 4096 + kc * 512:
                                           A_XT + osl * 4096 + (kc + 1) * 512],
                                  start=(kc == 0), stop=(kc == NCH - 1),
                                  waits=w(og_marks[kc], wob[kc],
                                          pbk_free[bi] if kc == 0 else None,
                                          bld if kc == 0 else None))
                if resid:
                    ev = self.ve(lambda e, T=T, ec=ec, bi=bi:
                                 e.tensor_scalar(T["ARF"][:, ec * 512:(ec + 1) * 512],
                                                 T[f"PBK{bi}"][:, 0:512],
                                                 T["COLS"][:, 8 + ec:9 + ec], None, ALU.add),
                                 waits=w(mmk, zsb_free if ec == 0 else None))
                    pbk_free[bi] = ev
                    ev_marks.append(ev)
                    s12 = self._emit_zstats(ec, A_ZB, A_ZSQ, ev,
                                            lambda T=T, ec=ec: T["ARF"][:, ec * 512:(ec + 1) * 512],
                                            zb_free, chain32_free)
                    stats_mm = s12
                else:
                    ev = self.ve(lambda e, T=T, ec=ec, bi=bi:
                                 e.tensor_scalar(T["ARB"][:, A_ZB + (ec % 2) * 512:
                                                          A_ZB + (ec % 2 + 1) * 512],
                                                 T[f"PBK{bi}"][:, 0:512],
                                                 T["COLS"][:, 8 + ec:9 + ec], None, ALU.add),
                                 waits=w(mmk, zb_free[ec % 2]))
                    pbk_free[bi] = ev
                    st = self.dma("gpsimd",
                                  lambda T=T, ec=ec, g=g, n=sp.get("loc_t", "LOCOUT"):
                                  T[n][:, ec, g * GT:(g + 1) * GT],
                                  lambda T=T, ec=ec: T["ARB"][:, A_ZB + (ec % 2) * 512:
                                                              A_ZB + (ec % 2 + 1) * 512],
                                  waits=w(ev), site="LOC")
                    zb_free[ec % 2] = st
            og_free[osl] = (self.PE, self.PE.n)
            if resid:
                zsb_free, chain32_free = self.emit_postln_g(sp, g, stats_mm, ev_marks)

    def _emit_zstats(self, ec, r_zb, r_zsq, ev, zsrc_fn, zb_free, chain32_free):
        T = self.T
        zb = self.ve(lambda e, T=T, ec=ec:
                     e.tensor_copy(T["ARB"][:, r_zb + (ec % 2) * 512:r_zb + (ec % 2 + 1) * 512],
                                   zsrc_fn()),
                     waits=w(ev, zb_free[ec % 2]))
        zq = self.ve(lambda e, T=T, ec=ec:
                     e.tensor_mul(T["ARB"][:, r_zsq + (ec % 2) * 512:r_zsq + (ec % 2 + 1) * 512],
                                  T["ARB"][:, r_zb + (ec % 2) * 512:r_zb + (ec % 2 + 1) * 512],
                                  T["ARB"][:, r_zb + (ec % 2) * 512:r_zb + (ec % 2 + 1) * 512]),
                     waits=w(zb))
        s1 = self.mm(lambda T=T: T["PA"][0:1, 0:512],
                     lambda T=T: T["CONSTB"][:, OC_ONES:OC_ONES + 1],
                     lambda T=T, ec=ec: T["ARB"][:, r_zb + (ec % 2) * 512:r_zb + (ec % 2 + 1) * 512],
                     start=(ec == 0), stop=(ec == NCH - 1),
                     waits=w(zb, chain32_free if ec == 0 else None))
        s2 = self.mm(lambda T=T: T["PA"][0:1, 512:1024],
                     lambda T=T: T["CONSTB"][:, OC_ONES:OC_ONES + 1],
                     lambda T=T, ec=ec: T["ARB"][:, r_zsq + (ec % 2) * 512:r_zsq + (ec % 2 + 1) * 512],
                     start=(ec == 0), stop=(ec == NCH - 1),
                     waits=w(zq, chain32_free if ec == 0 else None))
        zb_free[ec % 2] = s2
        return [s1, s2]

    def emit_postln_g(self, sp, g, stats_mm, z_ready):
        T = self.T
        R = self._postln_regs
        par = g % 2
        bf32, last = self.emit_ln_chain(g, "ln", stats_mm, row=0,
                                        b0=PZM + par * 512, b1=PZR + par * 512)
        mbc = self.mm(lambda T=T: T["PA"][:, 1024:1536],
                      lambda T=T: T["ONE1F"][0:1, 0:NP],
                      lambda T=T, par=par: T["ROWS"][0:1, PZM + par * 512:PZM + par * 512 + 512],
                      start=True, stop=True,
                      waits=w(last, self._pa_bc_free))
        rbcst = self.mm(lambda T=T: T["PBK2"][:, 0:512],
                        lambda T=T: T["ONE1F"][0:1, 0:NP],
                        lambda T=T, par=par: T["ROWS"][0:1, PZR + par * 512:PZR + par * 512 + 512],
                        start=True, stop=True,
                        waits=w(last, self._pbk2_free))
        final = sp.get("final", False)
        nxt = sp.get("next_chain")
        nstats = [None, None]
        zfree = None
        for f in range(NCH):
            sl = f % 2
            xl = self.dma("gpsimd",
                          lambda T=T, sl=sl, R=R: T["ARB"][:, R["xn"] + sl * 512:
                                                           R["xn"] + (sl + 1) * 512],
                          lambda T=T, f=f, g=g: T["XRES"][:, f, g * GT:(g + 1) * GT],
                          waits=w(R["xn_free"][sl]), site=f"XR{sl}")
            v1 = self.ve(lambda e, T=T, f=f, sl=sl:
                         e.tensor_sub(T["ARF"][:, 4096 + sl * 512:4096 + (sl + 1) * 512],
                                      T["ARF"][:, f * 512:(f + 1) * 512],
                                      T["PA"][:, 1024:1536]),
                         waits=w(mbc, z_ready[f], R["arfs_free"][sl]))
            zfree = v1
            v2 = self.ve(lambda e, T=T, sl=sl:
                         e.tensor_mul(T["ARF"][:, 5120 + sl * 512:5120 + (sl + 1) * 512],
                                      T["ARF"][:, 4096 + sl * 512:4096 + (sl + 1) * 512],
                                      T["PBK2"][:, 0:512]),
                         waits=w(rbcst, v1, R["arfs2_free"][sl]))
            R["arfs_free"][sl] = v2
            if final:
                v3 = self.ve(lambda e, T=T, sl=sl:
                             e.tensor_add(T["ARF"][:, 6144 + sl * 512:6144 + (sl + 1) * 512],
                                          T["ARF"][:, 5120 + sl * 512:5120 + (sl + 1) * 512],
                                          T["ARB"][:, R["xn"] + sl * 512:R["xn"] + (sl + 1) * 512]),
                             waits=w(v2, xl, R["arfo_free"][sl]))
                R["arfs2_free"][sl] = v3
                R["xn_free"][sl] = v3
                st = self.dma("gpsimd",
                              lambda T=T, f=f, g=g: T["OUT"][:, f, g * GT:(g + 1) * GT],
                              lambda T=T, sl=sl: T["ARF"][:, 6144 + sl * 512:6144 + (sl + 1) * 512],
                              waits=w(v3), site="XW")
                R["arfo_free"][sl] = st
            else:
                v3 = self.ve(lambda e, T=T, sl=sl, R=R:
                             e.tensor_add(T["ARB"][:, R["xnw"] + sl * 512:R["xnw"] + (sl + 1) * 512],
                                          T["ARF"][:, 5120 + sl * 512:5120 + (sl + 1) * 512],
                                          T["ARB"][:, R["xn"] + sl * 512:R["xn"] + (sl + 1) * 512]),
                             waits=w(v2, xl, R["xnw_free"][sl]))
                R["arfs2_free"][sl] = v3
                R["xn_free"][sl] = v3
                st = self.dma("gpsimd",
                              lambda T=T, f=f, g=g: T["XRES"][:, f, g * GT:(g + 1) * GT],
                              lambda T=T, sl=sl, R=R: T["ARB"][:, R["xnw"] + sl * 512:
                                                               R["xnw"] + (sl + 1) * 512],
                              waits=w(v3), site="XW")
                if nxt:
                    sqn = self.ve(lambda e, T=T, sl=sl, R=R:
                                  e.tensor_mul(T["ARB"][:, R["zsq"] + sl * 512:
                                                        R["zsq"] + (sl + 1) * 512],
                                               T["ARB"][:, R["xnw"] + sl * 512:
                                                        R["xnw"] + (sl + 1) * 512],
                                               T["ARB"][:, R["xnw"] + sl * 512:
                                                        R["xnw"] + (sl + 1) * 512]),
                                  waits=w(v3, R["zb_free"][sl]))
                    n1 = self.mm(lambda T=T: T["PA"][0:1, 0:512],
                                 lambda T=T: T["CONSTB"][:, OC_ONES:OC_ONES + 1],
                                 lambda T=T, sl=sl, R=R: T["ARB"][:, R["xnw"] + sl * 512:
                                                                 R["xnw"] + (sl + 1) * 512],
                                 start=(f == 0), stop=(f == NCH - 1),
                                 waits=w(v3, (self._chain0_banks_free, bf32) if f == 0 else None))
                    n2 = self.mm(lambda T=T: T["PA"][0:1, 512:1024],
                                 lambda T=T: T["CONSTB"][:, OC_ONES:OC_ONES + 1],
                                 lambda T=T, sl=sl, R=R: T["ARB"][:, R["zsq"] + sl * 512:
                                                                 R["zsq"] + (sl + 1) * 512],
                                 start=(f == 0), stop=(f == NCH - 1),
                                 waits=w(sqn, (self._chain0_banks_free, bf32) if f == 0 else None))
                    R["xnw_free"][sl] = [n2, st]
                    R["zb_free"][sl] = n2
                    nstats = [n1, n2]
                else:
                    R["xnw_free"][sl] = st
        self._pa_bc_free = zfree
        self._pbk2_free = (self.VE, self.VE.n)
        if not final and nxt:
            bf0, lst = self.emit_ln_chain(g, nxt, nstats, row=0)
            self._chain0_banks_free = bf0
            self.stats_mark = lst
            return zfree, w(bf32, bf0)
        return zfree, w(bf32, last)
    def emit_ffn(self, sp):
        self.fence()
        T = self.T
        l0 = sp["kind"] == "l0"
        nh = 24 if l0 else 32
        F_XS = 0
        F_HB = 4096
        F_WD = 4096 + nh * 512
        wd_w = nh * 1024 if l0 else 2 * nh * 128
        F_WUB = F_WD + wd_w
        wub_w = 2048 if l0 else 1024
        F_SG = F_WUB + 2 * wub_w
        F_ZB = F_SG + 1024
        F_ZSQ = F_ZB + 1024
        F_XT2 = F_ZSQ + 1024
        F_XN = F_XT2 + 1024
        F_XNW = F_XN + 1024
        assert F_XNW + 1024 <= ARB_W, (sp["kind"], F_XNW)
        wd_marks = [None] * nh
        if l0:
            for hc in range(nh):
                wd_marks[hc] = self.dma(
                    "sync",
                    lambda T=T, hc=hc, F_WD=F_WD:
                    T["ARB"][:, F_WD + hc * 1024:F_WD + (hc + 1) * 1024],
                    lambda T=T, n=sp["wd"], hc=hc: T[n][:, hc * 1024:(hc + 1) * 1024],
                    site="WD")
        wdsl_free = [None, None]
        zb_free = [None, None]
        xn_free = [None, None]
        xnw_free = [None, None]
        arfs_free = [None, None]
        arfs2_free = [None, None]
        arfo_free = [None, None]
        self._postln_regs = dict(zsq=F_ZSQ, xn=F_XN, xnw=F_XNW,
                                 zb_free=zb_free, xn_free=xn_free, xnw_free=xnw_free,
                                 arfs_free=arfs_free, arfs2_free=arfs2_free,
                                 arfo_free=arfo_free)
        self._pa_bc_free = None
        chain32_free = None
        wu_free = [None, None]
        hb_free = None
        zsb_free = None
        xs_free = None
        xt2_free = [None, None]
        sg_free = [None, None]
        pa_h_free = None
        pbk_ev = [None, None]
        self._cs_slot_free = [None, None, None, None]
        for g in range(NG):
            bc = self.mm(lambda T=T: T["PBK2"][:, 0:512],
                         lambda T=T: T["ONE1F"][0:1, 0:NP],
                         lambda T=T, g=g: T["ROWS"][0:1, RB1 + g * GT:RB1 + (g + 1) * GT],
                         start=True, stop=True,
                         waits=w(self._pbk2_free, self.stats_mark))
            tm = []
            for fc in range(NCH):
                sl = fc % 2
                ld = self.dma("gpsimd",
                              lambda T=T, sl=sl: T["ARB"][:, F_XT2 + sl * 512:F_XT2 + (sl + 1) * 512],
                              lambda T=T, fc=fc, g=g: T["XRES"][:, fc, g * GT:(g + 1) * GT],
                              waits=w(xt2_free[sl]), site=f"FX{sl}")
                mv = self.ve(lambda e, T=T, fc=fc, sl=sl:
                             e.tensor_tensor(T["ARB"][:, F_XS + fc * 512:F_XS + (fc + 1) * 512],
                                             T["ARB"][:, F_XT2 + sl * 512:F_XT2 + (sl + 1) * 512],
                                             T["PBK2"][:, 0:512], ALU.mult),
                             waits=w(ld, bc, xs_free if fc == 0 else None))
                xt2_free[sl] = mv
                tm.append(mv)
            self._pbk2_free = tm[-1]
            hmarks = []
            for hc in range(nh):
                wsl = hc % 2
                if l0:
                    wld = self.dma("sync",
                                   lambda T=T, wsl=wsl:
                                   T["ARB"][:, F_WUB + wsl * wub_w:F_WUB + wsl * wub_w + 1024],
                                   lambda T=T, n=sp["wg"], hc=hc:
                                   T[n][:, hc * 1024:(hc + 1) * 1024],
                                   waits=w(wu_free[wsl]), site=f"WU{wsl}")
                    wld2 = self.dma("sync",
                                    lambda T=T, wsl=wsl:
                                    T["ARB"][:, F_WUB + wsl * wub_w + 1024:F_WUB + wsl * wub_w + 2048],
                                    lambda T=T, n=sp["wu"], hc=hc:
                                    T[n][:, hc * 1024:(hc + 1) * 1024], site=f"WU{wsl}")
                else:
                    wld = self.dma("sync",
                                   lambda T=T, wsl=wsl:
                                   T["ARB"][:, F_WUB + wsl * wub_w:F_WUB + wsl * wub_w + 1024],
                                   lambda T=T, n=sp["wg"], hc=hc:
                                   T[n][:, hc * 1024:(hc + 1) * 1024],
                                   waits=w(wu_free[wsl]), site=f"WU{wsl}")
                    wld2 = wld
                nmm = 2 if l0 else 1
                gate_mark = up_mark = None
                cs_marks = [None, None]
                for im in range(nmm):
                    cslot = 5120 + ((hc % 2) * 2 + im) * 128
                    cm = None
                    for fc in range(NCH):
                        cm = self.mm(lambda T=T: T["PBK2"][0:1, 0:128],
                                     lambda T=T: T["CONSTB"][:, OC_ONES:OC_ONES + 1],
                                     lambda T=T, wsl=wsl, im=im, fc=fc, F_WUB=F_WUB, wub_w=wub_w:
                                     T["ARB"][:, F_WUB + wsl * wub_w + im * 1024 + fc * 128:
                                              F_WUB + wsl * wub_w + im * 1024 + (fc + 1) * 128],
                                     start=(fc == 0), stop=(fc == NCH - 1),
                                     waits=w((wld if im == 0 else wld2),
                                             self._pbk2_free) if fc == 0 else ())
                    csv = self.ve(lambda e, T=T, cslot=cslot:
                                  e.tensor_copy(T["ROWSB"][0:1, cslot:cslot + 128],
                                                T["PBK2"][0:1, 0:128]),
                                  waits=w(cm, self._cs_slot_free[(hc % 2) * 2 + im]))
                    self._pbk2_free = csv
                    cs_marks[im] = csv
                for im in range(nmm):
                    mmk = None
                    for fc in range(NCH):
                        mmk = self.mm(lambda T=T, im=im: T[f"PBK{im}"][:, 0:512],
                                      lambda T=T, wsl=wsl, im=im, fc=fc, F_WUB=F_WUB, wub_w=wub_w:
                                      T["ARB"][:, F_WUB + wsl * wub_w + im * 1024 + fc * 128:
                                               F_WUB + wsl * wub_w + im * 1024 + (fc + 1) * 128],
                                      lambda T=T, fc=fc:
                                      T["ARB"][:, F_XS + fc * 512:F_XS + (fc + 1) * 512],
                                      start=(fc == 0), stop=False,
                                      waits=w((wld if im == 0 else wld2) if fc == 0 else None,
                                              tm[fc],
                                              pbk_ev[im] if fc == 0 else None))
                    cslot = 5120 + ((hc % 2) * 2 + im) * 128
                    mmk = self.mm(lambda T=T, im=im: T[f"PBK{im}"][:, 0:512],
                                  lambda T=T, cslot=cslot:
                                  T["ROWSB"][0:1, cslot:cslot + 128],
                                  lambda T=T, g=g: T["ROWSB"][0:1, g * GT:(g + 1) * GT],
                                  start=False, stop=True,
                                  waits=w(self.stats_mark, cs_marks[im]))
                    self._cs_slot_free[(hc % 2) * 2 + im] = mmk
                    if im == 0:
                        gate_mark = mmk
                    else:
                        up_mark = mmk
                wu_free[wsl] = up_mark if l0 else gate_mark
                ssl = hc % 2
                if l0:
                    sg = self.sc(lambda e, T=T, ssl=ssl:
                                 e.activation(T["ARB"][:, F_SG + ssl * 512:F_SG + (ssl + 1) * 512],
                                              T["PBK0"][:, 0:512], AF.Silu),
                                 waits=w(gate_mark, sg_free[ssl]))
                    hm = self.ve(lambda e, T=T, hc=hc, ssl=ssl:
                                 e.tensor_tensor(T["ARB"][:, F_HB + hc * 512:F_HB + (hc + 1) * 512],
                                                 T["PBK1"][:, 0:512],
                                                 T["ARB"][:, F_SG + ssl * 512:F_SG + (ssl + 1) * 512],
                                                 ALU.mult),
                                 waits=w(sg, up_mark, hb_free if hc == 0 else None))
                    sg_free[ssl] = hm
                    pbk_ev = [sg, hm]
                else:
                    hm = self.sc(lambda e, T=T, hc=hc:
                                 e.activation(T["ARB"][:, F_HB + hc * 512:F_HB + (hc + 1) * 512],
                                              T["PBK0"][:, 0:512], AF.Gelu),
                                 waits=w(gate_mark, hb_free if hc == 0 else None))
                    sq = self.ve(lambda e, T=T, hc=hc, ssl=ssl:
                                 e.tensor_mul(T["ARB"][:, F_SG + ssl * 512:F_SG + (ssl + 1) * 512],
                                              T["ARB"][:, F_HB + hc * 512:F_HB + (hc + 1) * 512],
                                              T["ARB"][:, F_HB + hc * 512:F_HB + (hc + 1) * 512]),
                                 waits=w(hm, sg_free[ssl]))
                    smk = self.mm(lambda T=T: T["PA"][0:1, 0:512],
                                  lambda T=T: T["CONSTB"][:, OC_ONES:OC_ONES + 1],
                                  lambda T=T, ssl=ssl:
                                  T["ARB"][:, F_SG + ssl * 512:F_SG + (ssl + 1) * 512],
                                  start=(hc == 0), stop=(hc == nh - 1),
                                  waits=w(sq, pa_h_free if hc == 0 else None))
                    sg_free[ssl] = smk
                    pbk_ev = [hm, pbk_ev[1]]
                hmarks.append(hm)
            s2ev = None
            if not l0:
                par = g % 2
                sa = RSA if par == 0 else RSB
                sb = RSB if par == 0 else RSA
                t1 = self.ve(lambda e, T=T, sa=sa:
                             e.tensor_scalar(T["ROWS"][0:1, sa:sa + 512], T["PA"][0:1, 0:512],
                                             1.0 / 4096.0, None, ALU.mult),
                             waits=w((self.PE, self.PE.n)))
                pa_h_free = t1
                chain32_free = w(chain32_free, t1)
                t2 = self.sc(lambda e, T=T, sa=sa:
                             e.activation(T["ROWS"][0:1, sa:sa + 512],
                                          T["ROWS"][0:1, sa:sa + 512], AF.Sqrt),
                             waits=w(t1))
                t3 = self.ve(lambda e, T=T, sa=sa:
                             e.tensor_scalar(T["ROWS"][0:1, sa:sa + 512],
                                             T["ROWS"][0:1, sa:sa + 512],
                                             1e-8, None, ALU.add),
                             waits=w(t2))
                t4 = self.ve(lambda e, T=T, sa=sa, sb=sb:
                             e.reciprocal(T["ROWS"][0:1, sb:sb + 512],
                                          T["ROWS"][0:1, sa:sa + 512]),
                             waits=w(t3))
                s2bc = self.mm(lambda T=T: T["PBK2"][:, 0:512],
                               lambda T=T: T["ONE1F"][0:1, 0:NP],
                               lambda T=T, sb=sb: T["ROWS"][0:1, sb:sb + 512],
                               start=True, stop=True,
                               waits=w(t4, self._pbk2_free))
                s2ev = self.sc(lambda e, T=T:
                               e.copy(T["ARF"][:, 7168:7680], T["PBK2"][:, 0:512]),
                               waits=w(s2bc, self._s2_free))
                self._s2_free = None
                self._pbk2_free = s2ev
            stats_mm = [None, None]
            ev_marks = []
            zdn = [pbk_ev[0], pbk_ev[1]]
            for ec in range(NCH):
                bi = ec % 2
                if not l0:
                    dsl = ec % 2
                    wdl = self.dma("sync",
                                   lambda T=T, dsl=dsl, F_WD=F_WD, nh=nh:
                                   T["ARB"][:, F_WD + dsl * nh * 128:
                                            F_WD + (dsl + 1) * nh * 128],
                                   lambda T=T, n=sp["wd"], ec=ec, g=g, nh=nh:
                                   T[n][:, ec * nh * 128:(ec + 1) * nh * 128],
                                   waits=w(wdsl_free[dsl]), site=f"WD{dsl}")
                mmk = None
                for hc in range(nh):
                    if l0:
                        lhs = (lambda T=T, hc=hc, ec=ec, F_WD=F_WD:
                               T["ARB"][:, F_WD + hc * 1024 + ec * 128:
                                        F_WD + hc * 1024 + (ec + 1) * 128])
                        lw = wd_marks[hc] if ec < 2 else None
                    else:
                        dsl = ec % 2
                        lhs = (lambda T=T, hc=hc, dsl=dsl, F_WD=F_WD, nh=nh:
                               T["ARB"][:, F_WD + dsl * nh * 128 + hc * 128:
                                        F_WD + dsl * nh * 128 + (hc + 1) * 128])
                        lw = wdl if hc == 0 else None
                    mmk = self.mm(lambda T=T, bi=bi: T[f"PBK{bi}"][:, 0:512],
                                  lhs,
                                  lambda T=T, hc=hc:
                                  T["ARB"][:, F_HB + hc * 512:F_HB + (hc + 1) * 512],
                                  start=(hc == 0), stop=(hc == nh - 1),
                                  waits=w(hmarks[hc], lw,
                                          zdn[bi] if hc == 0 else None))
                if not l0:
                    wdsl_free[ec % 2] = mmk
                if l0:
                    ev = self.sc(lambda e, T=T, ec=ec, bi=bi:
                                 e.copy(T["ARF"][:, ec * 512:(ec + 1) * 512],
                                        T[f"PBK{bi}"][:, 0:512]),
                                 waits=w(mmk, zsb_free if ec == 0 else None))
                else:
                    ev = self.ve(lambda e, T=T, ec=ec, bi=bi:
                                 e.tensor_tensor(T["ARF"][:, ec * 512:(ec + 1) * 512],
                                                 T[f"PBK{bi}"][:, 0:512],
                                                 T["ARF"][:, 7168:7680], ALU.mult),
                                 waits=w(mmk, s2ev, zsb_free if ec == 0 else None))
                zdn[bi] = ev
                ev_marks.append(ev)
                stats_mm = self._emit_zstats(ec, F_ZB, F_ZSQ, ev,
                                             lambda T=T, ec=ec: T["ARF"][:, ec * 512:(ec + 1) * 512],
                                             zb_free, chain32_free)
            pbk_ev = zdn
            hb_free = (self.PE, self.PE.n)
            xs_free = hb_free
            if not l0:
                self._s2_free = (self.VE, self.VE.n)
            zsb_free, chain32_free = self.emit_postln_g(sp, g, stats_mm, ev_marks)

    def emit_dequant(self):
        self.fence()
        T = self.T
        ai_free = [None, None]
        ao_free = [None, None]
        for src, dst in [("FG", "WS0G"), ("FU", "WS0U"), ("FD", "WS0D")]:
            for blk in range(24):
                sl = blk % 2
                ld = self.dma("sync",
                              lambda T=T, sl=sl: T["ARI"][:, sl * 1024:(sl + 1) * 1024],
                              lambda T=T, src=src, blk=blk:
                              T[src][:, blk * 1024:(blk + 1) * 1024],
                              waits=w(ai_free[sl]), site=f"DQI{sl}")
                cv = self.sc(lambda e, T=T, sl=sl:
                             e.activation(T["ARB"][:, DQ_O + sl * 1024:DQ_O + (sl + 1) * 1024],
                                          T["ARI"][:, sl * 1024:(sl + 1) * 1024],
                                          AF.Copy, scale=1.0 / 127.0),
                             waits=w(ld, ao_free[sl]))
                ai_free[sl] = cv
                st = self.dma("gpsimd",
                              lambda T=T, dst=dst, blk=blk:
                              T[dst][:, blk * 1024:(blk + 1) * 1024],
                              lambda T=T, sl=sl:
                              T["ARB"][:, DQ_O + sl * 1024:DQ_O + (sl + 1) * 1024],
                              waits=w(cv), site="DQO")
                ao_free[sl] = st

    def emit_ternarize(self):
        T = self.T
        for src, dst in [("TUP", "WS1U"), ("TDN", "WS1D")]:
            self.fence()
            mz = self.ve(lambda e, T=T: e.memset(T["COLS"][:, 2:3], 0.0))
            af_free = [None, None]
            accm = mz
            for blk in range(16):
                sl = blk % 2
                ld = self.dma("sync",
                              lambda T=T, sl=sl: T["ARF"][:, sl * 2048:(sl + 1) * 2048],
                              lambda T=T, src=src, blk=blk:
                              T[src][:, blk * 2048:(blk + 1) * 2048],
                              waits=w(af_free[sl]), site=f"TNI{sl}")
                rd = self.ve(lambda e, T=T, sl=sl:
                             e.tensor_reduce(T["COLS"][:, 4:5],
                                             T["ARF"][:, sl * 2048:(sl + 1) * 2048],
                                             AXX, ALU.add, apply_absolute_value=True),
                             waits=w(ld, accm))
                accm = self.ve(lambda e, T=T:
                               e.tensor_add(T["COLS"][:, 2:3], T["COLS"][:, 2:3],
                                            T["COLS"][:, 4:5]),
                               waits=w(rd))
                af_free[sl] = rd
            tot = self.mm(lambda T=T: T["PBK2"][0:1, 0:1],
                          lambda T=T: T["COLS"][:, 2:3],
                          lambda T=T: T["CONSTF"][:, 4:5],
                          start=True, stop=True,
                          waits=w(accm, self._pbk2_free))
            tch = self.ve(lambda e, T=T:
                          e.tensor_scalar(T["ROWS"][0:1, RSA:RSA + 1], T["PBK2"][0:1, 0:1],
                                          0.5 / (4096.0 * 1024.0), None, ALU.mult),
                          waits=w(tot))
            tbc = self.mm(lambda T=T: T["PBK2"][:, 0:1],
                          lambda T=T: T["ONE1F"][0:1, 0:NP],
                          lambda T=T: T["ROWS"][0:1, RSA:RSA + 1],
                          start=True, stop=True, waits=w(tch))
            tcol = self.ve(lambda e, T=T: e.tensor_copy(T["COLS"][:, 3:4], T["PBK2"][:, 0:1]),
                           waits=w(tbc))
            self._pbk2_free = tcol
            af_free = [tcol, tcol]
            ab_free = [None, None]
            av_free = [None, None]
            wt_free = [None, None]
            for blk in range(16):
                sl = blk % 2
                ld = self.dma("sync",
                              lambda T=T, sl=sl: T["ARF"][:, sl * 2048:(sl + 1) * 2048],
                              lambda T=T, src=src, blk=blk:
                              T[src][:, blk * 2048:(blk + 1) * 2048],
                              waits=w(af_free[sl]), site=f"TNI{sl}")
                sg = self.sc(lambda e, T=T, sl=sl:
                             e.activation(T["ARB"][:, TN_S + sl * 2048:TN_S + (sl + 1) * 2048],
                                          T["ARF"][:, sl * 2048:(sl + 1) * 2048], AF.Sign),
                             waits=w(ld, ab_free[sl]))
                av = self.sc(lambda e, T=T, sl=sl:
                             e.activation(T["ARF"][:, 4096 + sl * 2048:4096 + (sl + 1) * 2048],
                                          T["ARF"][:, sl * 2048:(sl + 1) * 2048], AF.Abs),
                             waits=w(ld, av_free[sl]))
                af_free[sl] = [av, sg]
                mk0 = self.ve(lambda e, T=T, sl=sl:
                              e.tensor_scalar(T["ARF"][:, 4096 + sl * 2048:4096 + (sl + 1) * 2048],
                                              T["ARF"][:, 4096 + sl * 2048:4096 + (sl + 1) * 2048],
                                              T["COLS"][:, 3:4], None, ALU.subtract),
                              waits=w(av, tcol))
                mk = self.ve(lambda e, T=T, sl=sl:
                             e.tensor_single_scalar(T["ARF"][:, 4096 + sl * 2048:4096 + (sl + 1) * 2048],
                                                    T["ARF"][:, 4096 + sl * 2048:4096 + (sl + 1) * 2048],
                                                    0.0, ALU.is_gt),
                             waits=w(mk0))
                wt = self.ve(lambda e, T=T, sl=sl:
                             e.tensor_tensor(T["ARB"][:, TN_W + sl * 2048:TN_W + (sl + 1) * 2048],
                                             T["ARF"][:, 4096 + sl * 2048:4096 + (sl + 1) * 2048],
                                             T["ARB"][:, TN_S + sl * 2048:TN_S + (sl + 1) * 2048],
                                             ALU.mult),
                             waits=w(mk, sg, wt_free[sl]))
                av_free[sl] = wt
                ab_free[sl] = wt
                st = self.dma("gpsimd",
                              lambda T=T, dst=dst, blk=blk:
                              T[dst][:, blk * 2048:(blk + 1) * 2048],
                              lambda T=T, sl=sl:
                              T["ARB"][:, TN_W + sl * 2048:TN_W + (sl + 1) * 2048],
                              waits=w(wt), site="TNO")
                wt_free[sl] = st

    def emit_finalize(self):
        marks = self.barrier()
        for eng in ENGS:
            self.P.emit(eng, lambda e: e.nop(), waits=marks, inc=None)

    def build_nc(self):
        nc = bass.Bass("TRN2", target_bir_lowering=False, debug=False)
        T = self.T
        if self.launch == "F":
            decl_in = [("XN", [NP, NCH, S], F32), ("XQ", [NP, NCH, S], F32),
                       ("XF", [NP, NCH, 8 * S], F32),
                       ("WLOC", [PAIRS, NP, 3072], F32), ("BLOC", [PAIRS, NP, 3], F32),
                       ("WOUTL", [NP, NCH, 1024], F32), ("BOUTL", [NP, 8], F32),
                       ("WG", [PAIRS, NP, 3072], F32), ("BG", [PAIRS, NP, 3], F32),
                       ("WOUTG", [NP, NCH, 1024], F32), ("BOUTG", [NP, 8], F32),
                       ("WQKV1", [PAIRS, NP, 3072], F32), ("BQKV1", [PAIRS, NP, 3], F32),
                       ("WOUT1", [NP, NCH, 1024], F32), ("BOUT1", [NP, 8], F32),
                       ("M2", [H, NP, W2], BF16),
                       ("FG", [NP, 24576], I32), ("FU", [NP, 24576], I32),
                       ("FD", [NP, 24576], I32),
                       ("TUP", [NP, 32768], F32), ("TDN", [NP, 32768], F32)]
            decl_out = [("OUT", [NP, NCH, S], F32)]
            scratch = [("XRES", [NP, NCH, S], BF16), ("OALL", [NP, NCH, S], BF16),
                       ("LOCS", [NP, NCH, S], BF16),
                       ("WS0G", [NP, 24576], BF16), ("WS0U", [NP, 24576], BF16),
                       ("WS0D", [NP, 24576], BF16),
                       ("WS1U", [NP, 32768], BF16), ("WS1D", [NP, 32768], BF16)]
        elif self.launch == "A":
            decl_in = [("XU", [NP, NCH, S], F32),
                       ("WLOC", [PAIRS, NP, 3072], F32), ("BLOC", [PAIRS, NP, 3], F32),
                       ("WOUTL", [NP, NCH, 1024], F32), ("BOUTL", [NP, 8], F32)]
            decl_out = [("LOCOUT", [NP, NCH, S], BF16),
                        ("DBG1", [NP, 24576], BF16), ("DBG2", [NP, ROWS_W], F32),
                        ("DBG3", [NP, 16], F32), ("OALL", [NP, NCH, S], BF16)]
            scratch = []
        else:
            decl_in = [("XN", [NP, NCH, S], F32), ("LOCG", [NP, NCH, S], BF16),
                       ("WG", [PAIRS, NP, 3072], F32), ("BG", [PAIRS, NP, 3], F32),
                       ("WOUTG", [NP, NCH, 1024], F32), ("BOUTG", [NP, 8], F32),
                       ("WQKV1", [PAIRS, NP, 3072], F32), ("BQKV1", [PAIRS, NP, 3], F32),
                       ("WOUT1", [NP, NCH, 1024], F32), ("BOUT1", [NP, 8], F32),
                       ("M2", [H, NP, W2], BF16),
                       ("FG", [NP, 24576], I32), ("FU", [NP, 24576], I32),
                       ("FD", [NP, 24576], I32),
                       ("TUP", [NP, 32768], F32), ("TDN", [NP, 32768], F32)]
            decl_out = [("OUT", [NP, NCH, S], F32)]
            scratch = [("XRES", [NP, NCH, S], BF16), ("OALL", [NP, NCH, S], BF16),
                       ("WS0G", [NP, 24576], BF16), ("WS0U", [NP, 24576], BF16),
                       ("WS0D", [NP, 24576], BF16),
                       ("WS1U", [NP, 32768], BF16), ("WS1D", [NP, 32768], BF16)]
        const_in = [("CONSTB_D", [NP, CONSTB_W], BF16), ("CONSTF_D", [NP, 8], F32),
                    ("ONE1F_D", [1, NP], F32), ("ONE1B_D", [1, NP], BF16)]
        for n, sh, dt in decl_in + const_in:
            T[n] = nc.dram_tensor(n, sh, dt, kind="ExternalInput").ap()
        for n, sh, dt in decl_out:
            T[n] = nc.dram_tensor(n, sh, dt, kind="ExternalOutput").ap()
        for n, sh, dt in scratch:
            T[n] = nc.dram_tensor(n, sh, dt).ap()
        with ExitStack() as st:
            e_ = st.enter_context
            T["ARB"] = e_(nc.sbuf_tensor("ARB", [NP, ARB_W], BF16))
            T["ARF"] = e_(nc.sbuf_tensor("ARF", [NP, ARF_W], F32))
            if self.launch in ("B", "F"):
                T["ARI"] = e_(nc.sbuf_tensor("ARI", [NP, ARI_W], I32))
            T["ROWS"] = e_(nc.sbuf_tensor("ROWS", [NP, ROWS_W], F32))
            T["ROWSB"] = e_(nc.sbuf_tensor("ROWSB", [NP, ROWSB_W], BF16))
            T["COLS"] = e_(nc.sbuf_tensor("COLS", [NP, 16], F32))
            T["CONSTB"] = e_(nc.sbuf_tensor("CONSTB", [NP, CONSTB_W], BF16))
            T["CONSTF"] = e_(nc.sbuf_tensor("CONSTF", [NP, 8], F32))
            T["ONE1F"] = e_(nc.sbuf_tensor("ONE1F", [1, NP], F32))
            T["ONE1B"] = e_(nc.sbuf_tensor("ONE1B", [1, NP], BF16))
            T["SBIAS"] = e_(nc.sbuf_tensor("SBIAS", [NP, 8], F32))
            T["PA"] = e_(nc.psum_tensor("PA", [NP, 2048], F32))
            T["PBK0"] = e_(nc.psum_tensor("PBK0", [NP, 512], F32))
            T["PBK1"] = e_(nc.psum_tensor("PBK1", [NP, 512], F32))
            T["PBK2"] = e_(nc.psum_tensor("PBK2", [NP, 512], F32))
            T["PTRB"] = e_(nc.psum_tensor("PTRB", [NP, 256], BF16))
            sems = {}
            for name in self.P.chans:
                sems[name] = e_(nc.semaphore(name))
            blk = e_(nc.Block())
            with nc.allow_low_precision(reason="bf16 PE transpose of V tiles; all matmul accumulation is fp32"):
                self.P.build(sems, blk)
        return nc


def pack_pm(wv):
    Rr, C = wv.shape
    nch = Rr // NP
    return np.ascontiguousarray(
        wv.reshape(nch, NP, C).transpose(1, 0, 2).reshape(NP, nch * C))


def pack_qkv(Wv, bv):
    Wp = np.empty((PAIRS, NP, 3072), Wv.dtype)
    bp = np.empty((PAIRS, NP, 3), np.float32)
    for p in range(PAIRS):
        cols = np.concatenate([np.arange(blkk * D + p * 128, blkk * D + (p + 1) * 128)
                               for blkk in range(3)])
        Wsl = Wv[:, cols]
        Wp[p] = Wsl.reshape(NCH, NP, 384).transpose(1, 0, 2).reshape(NP, 3072)
        bp[p] = bv[cols].reshape(3, 128).T
    return np.ascontiguousarray(Wp), np.ascontiguousarray(bp)


def pack_ffn_w(wv, nhc):
    d_in, Hc = wv.shape
    out = np.empty((NP, nhc * 1024), wv.dtype)
    v = wv.reshape(NCH, NP, Hc)
    for hc in range(nhc):
        blkv = v[:, :, hc * 128:(hc + 1) * 128]
        out[:, hc * 1024:(hc + 1) * 1024] = blkv.transpose(1, 0, 2).reshape(NP, 1024)
    return np.ascontiguousarray(out)


def pack_down_w(wv, nhc):
    Hh, d_out = wv.shape
    return np.ascontiguousarray(
        wv.reshape(nhc, NP, d_out).transpose(1, 0, 2).reshape(NP, nhc * d_out))


def pack_down_w_ecmajor(wv, nhc):
    Hh, d_out = wv.shape
    v = wv.reshape(nhc, NP, NCH, 128)
    return np.ascontiguousarray(
        v.transpose(1, 2, 0, 3).reshape(NP, NCH * nhc * 128))


def make_consts():
    Cc = np.zeros((NP, CONSTB_W), np.float32)
    Cc[:, OC_ONES] = 1.0
    Cc[0:64, OC_EA + 64] = 1.0
    mask = np.zeros((NP, 128), np.float32)
    for g8 in range(16):
        mask[g8 * 8:(g8 + 1) * 8, g8 * 8:(g8 + 1) * 8] = 1.0
    Cc[:, OC_MASK:OC_MASK + 512] = np.tile(mask, (1, 4))
    Cc[0:64, OC_EYE:OC_EYE + 64] = np.eye(64)
    Cc[64:128, OC_EYE:OC_EYE + 64] = np.eye(64)
    for r in range(NP):
        Cc[r, OC_MASK8 + 112 + r // 8] = 1.0
    constf = np.zeros((NP, 8), np.float32)
    constf[:, 0] = 1e-5
    constf[:, 2] = -1.0
    constf[:, 3] = 0.125
    constf[:, 4] = 1.0
    return (Cc.astype(bfloat16), constf,
            np.ones((1, NP), np.float32), np.ones((1, NP), bfloat16))


def make_m2(rel_bias):
    uu = np.arange(W2)[None, :]
    dj = np.arange(NP)[:, None]
    idx = np.clip(uu - dj - 3071, 0, 2048)
    out = np.empty((H, NP, W2), bfloat16)
    for h in range(H):
        out[h] = rel_bias[:, h].astype(bfloat16)[idx]
    return out


def host_prep(inputs):
    cb, cf, o1f, o1b = make_consts()
    consts = dict(CONSTB_D=cb, CONSTF_D=cf, ONE1F_D=o1f, ONE1B_D=o1b)
    x = np.asarray(inputs["L0_attn_x"])
    wl, bl = pack_qkv(np.ascontiguousarray(np.asarray(inputs["L0_attn_w_in_l"]).T),
                      np.asarray(inputs["L0_attn_b_in_l"]))
    woutl = pack_pm(np.ascontiguousarray(np.asarray(inputs["L0_attn_w_out_l"]).T)
                    ).reshape(NP, NCH, 1024)
    boutl = np.asarray(inputs["L0_attn_b_out_l"]).reshape(NCH, NP).T.copy()
    mapsA = []
    for b in range(8):
        xb = x[b].reshape(NCH, GT, D).transpose(1, 0, 2).reshape(S, D)
        xu = pack_pm(np.ascontiguousarray(xb.T)).reshape(NP, NCH, S)
        mapsA.append(dict(XU=xu, WLOC=wl, BLOC=bl, WOUTL=woutl, BOUTL=boutl, **consts))
    wg, bg = pack_qkv(np.ascontiguousarray(np.asarray(inputs["L0_attn_w_in_g"]).T),
                      np.asarray(inputs["L0_attn_b_in_g"]))
    woutg = pack_pm(np.ascontiguousarray(np.asarray(inputs["L0_attn_w_out_g"]).T)
                    ).reshape(NP, NCH, 1024)
    boutg = np.asarray(inputs["L0_attn_b_out_g"]).reshape(NCH, NP).T.copy()
    w1, b1 = pack_qkv(np.ascontiguousarray(np.asarray(inputs["L1_attn_W_qkv"])),
                      np.asarray(inputs["L1_attn_b_qkv"]))
    wout1 = pack_pm(np.ascontiguousarray(np.asarray(inputs["L1_attn_W_out"]))
                    ).reshape(NP, NCH, 1024)
    bout1 = np.asarray(inputs["L1_attn_b_out"]).reshape(NCH, NP).T.copy()
    m2 = make_m2(np.asarray(inputs["L1_attn_rel_bias"]))
    fg = pack_ffn_w(np.ascontiguousarray(np.asarray(inputs["L0_ffn_w_gate"]).T), 24)
    fu = pack_ffn_w(np.ascontiguousarray(np.asarray(inputs["L0_ffn_w_up"]).T), 24)
    fd = pack_down_w(np.ascontiguousarray(np.asarray(inputs["L0_ffn_w_down"]).T), 24)
    tup = pack_ffn_w(np.ascontiguousarray(np.asarray(inputs["L1_ffn_w_up"]).T), 32)
    tdn = pack_down_w_ecmajor(np.ascontiguousarray(np.asarray(inputs["L1_ffn_w_down"]).T), 32)
    sharedB = dict(WG=wg, BG=bg, WOUTG=woutg, BOUTG=boutg, WQKV1=w1, BQKV1=b1,
                   WOUT1=wout1, BOUT1=bout1, M2=m2, FG=fg, FU=fu, FD=fd,
                   TUP=tup, TDN=tdn, **consts)
    return mapsA, sharedB, x


def mapsB_from_loc(loc_parts, x, sharedB):
    mapsB = []
    for c in range(8):
        locg = np.empty((NP, NCH, S), bfloat16)
        for b in range(8):
            locg[:, :, b * GT:(b + 1) * GT] = loc_parts[b][:, :, c::8]
        xn = pack_pm(np.ascontiguousarray(x[c].T)).reshape(NP, NCH, S)
        mapsB.append(dict(XN=xn, LOCG=locg, **sharedB))
    return mapsB


def out_unpack(outs):
    res = np.empty((8, S, D), np.float32)
    for c in range(8):
        fm = np.asarray(outs[c]).transpose(1, 0, 2).reshape(D, S)
        res[c] = fm.T
    return res


def build_progA(stage=99):
    k = Kern("A")
    k.emit_consts()
    if stage >= 1:
        k.emit_stats_pass("XU", src_f32=True, chain="ln")
    if stage >= 2:
        k.emit_attention(dict(wq="WLOC", bq="BLOC", src="XU", src_kind="f32_fold",
                              diag=True, eb=None, wout="WOUTL", bout="BOUTL",
                              out="loc", stage=stage,
                              dbg=int(os.environ.get("BASSKERN_DBG", "0"))))
    k.emit_finalize()
    return k


def build_progB(stage=99):
    k = Kern("B")
    k.emit_consts()
    k.emit_x_init()
    if stage >= 2:
        k.emit_attention(dict(wq="WG", bq="BG", src="LOCG", src_kind="bf16",
                              diag=False, eb=None, wout="WOUTG", bout="BOUTG",
                              out="resid", next_chain="ln"))
    if stage >= 3:
        k.emit_dequant()
    if stage >= 4:
        k.emit_ffn(dict(kind="l0", wg="WS0G", wu="WS0U", wd="WS0D",
                        out="resid", next_chain="ln"))
    k.switch_epoch("2")
    if stage >= 5:
        k.emit_attention(dict(wq="WQKV1", bq="BQKV1", src="XRES", src_kind="bf16_fold",
                              diag=False, eb="M2", wout="WOUT1", bout="BOUT1",
                              out="resid", next_chain="ln_s"))
    if stage >= 6:
        k.emit_ternarize()
    if stage >= 7:
        k.emit_ffn(dict(kind="l1", wg="WS1U", wu=None, wd="WS1D",
                        out="resid", next_chain=None, final=True))
    k.emit_finalize()
    return k


def build_progF():
    k = Kern("F")
    k.emit_consts()
    k.emit_x_init()
    k.emit_attention_l2(dict(wq="WLOC", bq="BLOC", wout="WOUTL", bout="BOUTL",
                             out="loc", loc_t="LOCS"))
    k.switch_epoch("1")
    k.emit_attention(dict(wq="WG", bq="BG", src="LOCS", src_kind="bf16",
                          diag=False, eb=None, wout="WOUTG", bout="BOUTG",
                          out="resid", next_chain="ln"))
    k.emit_dequant()
    k.emit_ffn(dict(kind="l0", wg="WS0G", wu="WS0U", wd="WS0D",
                    out="resid", next_chain="ln"))
    k.switch_epoch("2")
    k.emit_attention(dict(wq="WQKV1", bq="BQKV1", src="XRES", src_kind="bf16_fold",
                          diag=False, eb="M2", wout="WOUT1", bout="BOUT1",
                          out="resid", next_chain="ln_s"))
    k.emit_ternarize()
    k.emit_ffn(dict(kind="l1", wg="WS1U", wu=None, wd="WS1D",
                    out="resid", next_chain=None, final=True))
    k.emit_finalize()
    return k


def maps_fused(inputs):
    mapsA, sharedB, x = host_prep(inputs)
    XF = np.ascontiguousarray(
        np.concatenate([mapsA[b]["XU"] for b in range(8)], axis=2))
    shared = dict(sharedB)
    shared.update(XF=XF, WLOC=mapsA[0]["WLOC"], BLOC=mapsA[0]["BLOC"],
                  WOUTL=mapsA[0]["WOUTL"], BOUTL=mapsA[0]["BOUTL"])
    maps = []
    for c in range(8):
        xqf = x[:, c * GT:(c + 1) * GT, :].reshape(8 * GT, D)
        xq = pack_pm(np.ascontiguousarray(xqf.T)).reshape(NP, NCH, S)
        xn = pack_pm(np.ascontiguousarray(x[c].T)).reshape(NP, NCH, S)
        maps.append(dict(XN=xn, XQ=xq, **shared))
    return maps


_NC_CACHE = {}


def kernel(**inputs):
    maps = maps_fused(inputs)
    if "F" not in _NC_CACHE:
        _NC_CACHE["F"] = build_progF().build_nc()
    res = run_bass_kernel_spmd(_NC_CACHE["F"], maps, core_ids=list(range(8)))
    return out_unpack([r["OUT"] for r in res.results])
